# version 04 (final)
# baseline (speedup 1.0000x reference)
import numpy as np
from contextlib import ExitStack

import concourse.bass as bass
import concourse.mybir as mybir
from concourse.bass_utils import run_bass_kernel_spmd
from concourse.alu_op_type import AluOpType as OP

F32 = mybir.dt.float32
F32R = mybir.dt.float32r
BF16 = mybir.dt.bfloat16
AF = mybir.ActivationFunctionType
AX = mybir.AxisListType

B, T, D = 32, 2048, 1024
NCORES = 8
BPC = B // NCORES
H_FF = 4096
LN_EPS = 1e-5


class Env:
    def __init__(self, nc, stack):
        self.nc = nc
        self.stack = stack
        self.semstack = stack
        self.sems = {}
        self.base = {}
        self.ser_last = {}
        self.semgen = {}

    def sem(self, name):
        if name not in self.sems:
            gen = self.semgen.get(name, 0)
            self.semgen[name] = gen + 1
            phys = name if gen == 0 else name + "~" + str(gen)
            self.sems[name] = self.semstack.enter_context(
                self.nc.semaphore(phys))
            self.base[name] = 0
        return self.sems[name]

    def free_phase(self, prefixes):
        names = [n for n in self.sems
                 if any(n.startswith(p) for p in prefixes)]
        if not names:
            return
        self.nc.clear_and_free_semaphores([self.sems[n] for n in names])
        self.nc.all_engine_barrier()
        for n in names:
            del self.sems[n]
            del self.base[n]
        for k in self.ser_last:
            self.ser_last[k] = None

    def sbuf(self, name, shape, dtype, stack=None):
        return (stack or self.stack).enter_context(
            self.nc.sbuf_tensor(name, shape, dtype))

    def psum(self, name, shape, dtype, stack=None):
        return (stack or self.stack).enter_context(
            self.nc.psum_tensor(name, shape, dtype))


class Ser:

    def __init__(self, env, name, eng):
        self._env = env
        self._name = "ser_" + name
        self._eng = eng
        self._sem = env.sem(self._name)

    def wait_ge(self, sem, val):
        self._eng.wait_ge(sem, val)

    def dma_start(self, dst, src, **kw):
        last = self._env.ser_last.get(self._name)
        if last is not None:
            self._eng.wait_ge(*last)
        return self._eng.dma_start(dst, src, **kw)

    def __getattr__(self, fname):
        eng_f = getattr(self._eng, fname)

        def wrapped(*a, inc=None, **kw):
            last = self._env.ser_last.get(self._name)
            if last is not None:
                self._eng.wait_ge(*last)
            ins = eng_f(*a, **kw)
            if inc is None:
                n = self._env.base[self._name] + 1
                self._env.base[self._name] = n
                ins.then_inc(self._sem, 1)
                self._env.ser_last[self._name] = (self._sem, n)
            else:
                sem, amount, after = inc
                ins.then_inc(sem, amount)
                self._env.ser_last[self._name] = (sem, after)
            return ins
        return wrapped


class Dma2:

    def __init__(self, env, name):
        self.env = env
        self.name = name
        self.s = [env.sem(name + "0"), env.sem(name + "1")]
        self.n = [env.base[name + "0"] // 16, env.base[name + "1"] // 16]

    def start(self, eng, dst, src, idx, **kw):
        eng.dma_start(dst, src, **kw).then_inc(self.s[idx % 2], 16)
        self.n[idx % 2] += 1

    def wait(self, eng, idx):
        eng.wait_ge(self.s[idx % 2],
                    self.env.base[self.name + str(idx % 2)] + 16 * (idx // 2 + 1))

    def wait_all(self, eng):
        eng.wait_ge(self.s[0], 16 * self.n[0])
        eng.wait_ge(self.s[1], 16 * self.n[1])

    def done(self):
        self.env.base[self.name + "0"] = 16 * self.n[0]
        self.env.base[self.name + "1"] = 16 * self.n[1]


def emit_consts(env):
    nc = env.nc
    env.epscol = env.sbuf("epscol", [128, 1], F32)
    env.ident_bf = env.sbuf("ident_bf", [128, 128], BF16)
    env.ident_f = env.sbuf("ident_f", [128, 128], F32)
    env.onescol_bf = env.sbuf("onescol_bf", [128, 1], BF16)
    env.onesrow_f = env.sbuf("onesrow_f", [1, 128], F32)
    env.onesrow_bf = env.sbuf("onesrow_bf", [1, 128], BF16)
    with nc.Block("consts") as blk:
        @blk.gpsimd
        def _(g):
            g = Ser(env, "g", g)
            g.memset(env.ident_bf[:, :], 1.0)
            g.memset(env.ident_f[:, :], 1.0)
            g.memset(env.onescol_bf[:, :], 1.0)
            g.memset(env.onesrow_f[:, :], 1.0)
            g.memset(env.onesrow_bf[:, :], 1.0)
            g.memset(env.epscol[:, :], LN_EPS)
            g.affine_select(env.ident_bf[:, :], env.ident_bf[:, :], [[1, 128]],
                            compare_op=OP.is_equal, fill=0.0,
                            base=0, channel_multiplier=-1)
            g.affine_select(env.ident_f[:, :], env.ident_f[:, :], [[1, 128]],
                            compare_op=OP.is_equal, fill=0.0,
                            base=0, channel_multiplier=-1)
    nc.const_aps.register(F32, LN_EPS, env.epscol[:, 0:1])


def emit_vec_to_cols(env, vec_ap, n, name, want_f32=True, want_bf16=False):
    nc = env.nc
    k = n // 128
    out = {}
    if want_f32:
        out["f32"] = env.sbuf(name + "_cf", [128, k], F32)
    if want_bf16:
        out["bf16"] = env.sbuf(name + "_cb", [128, k], BF16)
    with ExitStack() as ls:
        row = env.sbuf(name + "_row", [1, n], F32, stack=ls)
        ps = env.psum(name + "_ps", [128, max(k, 2)], F32, stack=ls)
        d = env.sem("v2c_d")
        m = env.sem("v2c_m")
        with nc.Block("v2c_" + name) as blk:
            @blk.sync
            def _(s):
                s.dma_start(row[:, :], vec_ap.rearrange("(o n) -> o n", o=1)
                            ).then_inc(d, 16)
            @blk.tensor
            def _(t):
                t.wait_ge(d, env.base["v2c_d"] + 16)
                ins = None
                for c in range(k):
                    ins = t.matmul(ps[:, c:c + 1], row[:, c * 128:(c + 1) * 128],
                                   env.ident_f[0:1, 0:1], is_transpose=True,
                                   start=True, stop=True, skip_group_check=True)
                ins.then_inc(m, 1)
            @blk.vector
            def _(v):
                v = Ser(env, "v", v)
                v.wait_ge(m, env.base["v2c_m"] + 1)
                if want_f32:
                    v.tensor_copy(out["f32"][:, :], ps[:, 0:k])
                if want_bf16:
                    v.tensor_copy(out["bf16"][:, :], ps[:, 0:k])
        env.base["v2c_d"] += 16
        env.base["v2c_m"] += 1
    return out



def emit_ternarize(env, w_ap, rows, cols, out_dram_blocks, tag):
    nc = env.nc
    ntile = rows // 128
    nw = cols // 1024
    nob = cols // 128
    with ExitStack() as ls:
        wt = env.sbuf(tag + "_wt", [128, 2048], F32, stack=ls)
        at = env.sbuf(tag + "_at", [128, 1024], F32, stack=ls)
        st_ = env.sbuf(tag + "_st", [128, 1024], F32, stack=ls)
        mt = env.sbuf(tag + "_mt", [128, 1024], F32, stack=ls)
        gacc = env.sbuf(tag + "_gacc", [128, 2], F32, stack=ls)
        grow = env.sbuf(tag + "_grow", [1, 2], F32, stack=ls)
        thr = env.sbuf(tag + "_thr", [128, 1], F32, stack=ls)
        qbuf = env.sbuf(tag + "_qbuf", [128, ntile * cols], BF16, stack=ls)
        stage = env.sbuf(tag + "_stage", [128, 2 * ntile * 128], BF16, stack=ls)
        ps_t = env.psum(tag + "_ps_t", [128, 1024], BF16, stack=ls)
        ps_r = env.psum(tag + "_ps_r", [128, 128], F32, stack=ls)
        vs = env.sem("tern_v")
        ss = env.sem("tern_s")
        ts = env.sem("tern_t")
        dl = Dma2(env, "tern_d")
        do = Dma2(env, "tern_o")
        NG = ntile * nw
        b = {"v": env.base["tern_v"]}
        with nc.Block(tag + "_gamma") as blk:
            @blk.sync
            def _(s):
                for g in range(NG):
                    if g >= 2:
                        s.wait_ge(vs, b["v"] + g - 1)
                    i, j = divmod(g, nw)
                    dl.start(s, wt[:, (g % 2) * 1024:(g % 2) * 1024 + 1024],
                             w_ap[i * 128:(i + 1) * 128, j * 1024:(j + 1) * 1024], g)
            @blk.vector
            def _(v):
                v = Ser(env, "v", v)
                for g in range(NG):
                    dl.wait(v, g)
                    v.reduce_sum(gacc[:, 1:2],
                                 wt[:, (g % 2) * 1024:(g % 2) * 1024 + 1024],
                                 axis=AX.X, apply_absolute_value=True)
                    if g == 0:
                        v.tensor_copy(gacc[:, 0:1], gacc[:, 1:2],
                                      inc=(vs, 1, b["v"] + 1))
                    else:
                        v.tensor_tensor(gacc[:, 0:1], gacc[:, 0:1], gacc[:, 1:2],
                                        op=OP.add, inc=(vs, 1, b["v"] + g + 1))
        dl.done()
        env.base["tern_v"] += NG
        with nc.Block(tag + "_gth") as blk:
            @blk.tensor
            def _(t):
                t.wait_ge(vs, env.base["tern_v"])
                t.transpose(ps_r[0:1, 0:128], gacc[:, 0:1], env.ident_f[:, :]
                            ).then_inc(ts, 1)
            @blk.vector
            def _(v):
                v = Ser(env, "v", v)
                v.wait_ge(ts, env.base["tern_t"] + 1)
                v.reduce_sum(grow[0:1, 0:1], ps_r[0:1, 0:128], axis=AX.X)
                v.tensor_scalar(grow[0:1, 1:2], grow[0:1, 0:1],
                                0.5 / (rows * cols), 0.5e-6,
                                op0=OP.mult, op1=OP.add,
                                inc=(vs, 1, env.base["tern_v"] + 1))
        env.base["tern_t"] += 1
        env.base["tern_v"] += 1
        with nc.Block(tag + "_gth2") as blk:
            @blk.tensor
            def _(t):
                t.wait_ge(vs, env.base["tern_v"])
                t.matmul(ps_r[:, 0:1], env.onesrow_f[:, :], grow[0:1, 1:2],
                         start=True, stop=True).then_inc(ts, 1)
            @blk.vector
            def _(v):
                v = Ser(env, "v", v)
                v.wait_ge(ts, env.base["tern_t"] + 1)
                v.tensor_copy(thr[:, :], ps_r[:, 0:1],
                              inc=(vs, 1, env.base["tern_v"] + 1))
        env.base["tern_t"] += 1
        env.base["tern_v"] += 1
        b = {"v": env.base["tern_v"], "s": env.base["tern_s"]}
        dl = Dma2(env, "tern_d")
        with nc.Block(tag + "_tern") as blk:
            @blk.sync
            def _(s):
                for g in range(NG):
                    if g >= 2:
                        s.wait_ge(vs, b["v"] + g - 1)
                    i, j = divmod(g, nw)
                    dl.start(s, wt[:, (g % 2) * 1024:(g % 2) * 1024 + 1024],
                             w_ap[i * 128:(i + 1) * 128, j * 1024:(j + 1) * 1024], g)
            @blk.scalar
            def _(sc):
                sc = Ser(env, "s", sc)
                for g in range(NG):
                    dl.wait(sc, g)
                    if g >= 1:
                        sc.wait_ge(vs, b["v"] + g)
                    src = wt[:, (g % 2) * 1024:(g % 2) * 1024 + 1024]
                    sc.activation(at[:, :], src, AF.Abs)
                    sc.sign(st_[:, :], src, inc=(ss, 1, b["s"] + g + 1))
            @blk.vector
            def _(v):
                v = Ser(env, "v", v)
                for g in range(NG):
                    v.wait_ge(ss, b["s"] + g + 1)
                    v.tensor_scalar(mt[:, :], at[:, :], thr[:, 0:1], None,
                                    op0=OP.is_gt)
                    v.tensor_tensor(qbuf[:, g * 1024:(g + 1) * 1024],
                                    mt[:, :], st_[:, :], op=OP.mult,
                                    inc=(vs, 1, b["v"] + g + 1))
        dl.done()
        env.base["tern_s"] += NG
        env.base["tern_v"] += NG
        b = {"v": env.base["tern_v"], "t": env.base["tern_t"]}
        nsub = (ntile + 7) // 8
        with nc.Block(tag + "_xp") as blk:
            @blk.tensor
            def _(t):
                for oc in range(nob):
                    for sub in range(nsub):
                        gs = oc * nsub + sub
                        if gs >= 1:
                            t.wait_ge(vs, b["v"] + gs)
                        lo, hi = sub * 8, min(ntile, sub * 8 + 8)
                        ins = None
                        for i in range(lo, hi):
                            ins = t.matmul(
                                ps_t[:, (i - lo) * 128:(i - lo) * 128 + 128],
                                qbuf[:, i * cols + oc * 128:i * cols + oc * 128 + 128],
                                env.ident_bf[:, :], is_transpose=True,
                                start=True, stop=True, skip_group_check=True)
                        ins.then_inc(ts, 1)
            @blk.vector
            def _(v):
                v = Ser(env, "v", v)
                for oc in range(nob):
                    for sub in range(nsub):
                        gs = oc * nsub + sub
                        v.wait_ge(ts, b["t"] + gs + 1)
                        if oc >= 2 and sub == 0:
                            do.wait(v, oc - 2)
                        lo, hi = sub * 8, min(ntile, sub * 8 + 8)
                        v.tensor_copy(
                            stage[:, (oc % 2) * ntile * 128 + lo * 128:
                                  (oc % 2) * ntile * 128 + hi * 128],
                            ps_t[:, 0:(hi - lo) * 128],
                            inc=(vs, 1, b["v"] + gs + 1))
            @blk.gpsimd
            def _(g):
                g = Ser(env, "g", g)
                for oc in range(nob):
                    g.wait_ge(vs, b["v"] + (oc + 1) * nsub)
                    do.start(g, out_dram_blocks(oc),
                             stage[:, (oc % 2) * ntile * 128:
                                   (oc % 2) * ntile * 128 + ntile * 128], oc)
                do.wait_all(g)
        env.base["tern_t"] += nob * nsub
        env.base["tern_v"] += nob * nsub
        do.done()


def emit_l0ffn_prep(env, wfc_ap, lnw_ap, lnb_ap, wproj_ap):
    nc = env.nc
    env.wfcT_q = nc.dram_tensor("wfcT_q", [32, 8, 128, 128], BF16,
                                kind="Internal").ap()
    env.wprojT_raw = nc.dram_tensor("wprojT_raw", [H_FF, D], BF16,
                                    kind="Internal").ap()
    env.wprojT_s = nc.dram_tensor("wprojT_s", [H_FF, D], BF16,
                                  kind="Internal").ap()
    env.lnWp_hbm = nc.dram_tensor("lnWp_hbm", [1, 2 * D], BF16,
                                  kind="Internal").ap()
    env.lnw_cols = emit_vec_to_cols(env, lnw_ap, H_FF, "lnw")["f32"]
    env.lnb_cols = emit_vec_to_cols(env, lnb_ap, H_FF, "lnb",
                                    want_f32=False, want_bf16=True)["bf16"]

    def wfc_dst(oc):
        return env.wfcT_q[:, oc].rearrange("c p n -> p c n")
    emit_ternarize(env, wfc_ap, H_FF, D, wfc_dst, "tfc")

    def wproj_dst(oc):
        return env.wprojT_raw[oc * 128:(oc + 1) * 128, :].rearrange(
            "p (c n) -> p c n", c=8)
    emit_ternarize(env, wproj_ap, D, H_FF, wproj_dst, "tpj")

    with ExitStack() as ls:
        raw = env.sbuf("scp_raw", [128, 2048], BF16, stack=ls)
        stg = env.sbuf("scp_stg", [128, 2048], BF16, stack=ls)
        ps_a = env.psum("scp_psa", [128, 512], F32, stack=ls)
        ps_b = env.psum("scp_psb", [128, 512], F32, stack=ls)
        ps_c = env.psum("scp_psc", [128, 512], F32, stack=ls)
        ps_d = env.psum("scp_psd", [128, 512], F32, stack=ls)
        lnWp = env.sbuf("scp_lnWp", [1, 2 * D], BF16, stack=ls)
        ss = env.sem("scp_s")
        ts = env.sem("scp_t")
        dl = Dma2(env, "scp_d")
        do = Dma2(env, "scp_o")
        b = {"s": env.base["scp_s"], "t": env.base["scp_t"]}
        with nc.Block("l0scale") as blk:
            @blk.sync
            def _(s):
                for hc in range(32):
                    if hc >= 2:
                        s.wait_ge(ts, b["t"] + 2 * (hc - 1))
                    dl.start(s, raw[:, (hc % 2) * 1024:(hc % 2) * 1024 + 1024],
                             env.wprojT_raw[hc * 128:(hc + 1) * 128, :], hc)
            @blk.scalar
            def _(sc):
                sc = Ser(env, "s", sc)
                for hc in range(32):
                    dl.wait(sc, hc)
                    if hc >= 2:
                        do.wait(sc, hc - 2)
                    sc.activation(stg[:, (hc % 2) * 1024:(hc % 2) * 1024 + 1024],
                                  raw[:, (hc % 2) * 1024:(hc % 2) * 1024 + 1024],
                                  AF.Copy, scale=env.lnw_cols[:, hc:hc + 1],
                                  inc=(ss, 1, b["s"] + hc + 1))
            @blk.tensor
            def _(t):
                for hc in range(32):
                    t.wait_ge(ss, b["s"] + hc + 1)
                    rawb = raw[:, (hc % 2) * 1024:(hc % 2) * 1024 + 1024]
                    stgb = stg[:, (hc % 2) * 1024:(hc % 2) * 1024 + 1024]
                    t.matmul(ps_a[0:1, 0:512], env.lnb_cols[:, hc:hc + 1],
                             rawb[:, 0:512], start=(hc == 0), stop=(hc == 31),
                             skip_group_check=True)
                    t.matmul(ps_b[0:1, 0:512], env.lnb_cols[:, hc:hc + 1],
                             rawb[:, 512:1024], start=(hc == 0), stop=(hc == 31),
                             skip_group_check=True)
                    t.matmul(ps_c[0:1, 0:512], env.onescol_bf[:, 0:1],
                             stgb[:, 0:512], start=(hc == 0), stop=(hc == 31),
                             skip_group_check=True)
                    t.matmul(ps_d[0:1, 0:512], env.onescol_bf[:, 0:1],
                             stgb[:, 512:1024], start=(hc == 0), stop=(hc == 31),
                             skip_group_check=True).then_inc(ts, 2)
            @blk.gpsimd
            def _(g):
                g = Ser(env, "g", g)
                for hc in range(32):
                    g.wait_ge(ss, b["s"] + hc + 1)
                    do.start(g, env.wprojT_s[hc * 128:(hc + 1) * 128, :],
                             stg[:, (hc % 2) * 1024:(hc % 2) * 1024 + 1024], hc)
                g.wait_ge(ss, b["s"] + 33)
                do.start(g, env.lnWp_hbm[0:1, :], lnWp[0:1, :], 32)
                do.wait_all(g)
            @blk.vector
            def _(v):
                v = Ser(env, "v", v)
                v.wait_ge(ts, b["t"] + 64)
                v.tensor_copy(lnWp[0:1, 0:512], ps_a[0:1, 0:512])
                v.tensor_copy(lnWp[0:1, 512:1024], ps_b[0:1, 0:512])
                v.tensor_copy(lnWp[0:1, 1024:1536], ps_c[0:1, 0:512])
                v.tensor_copy(lnWp[0:1, 1536:2048], ps_d[0:1, 0:512],
                              inc=(ss, 1, b["s"] + 33))
        dl.done()
        do.done()
        env.base["scp_s"] += 33
        env.base["scp_t"] += 64



def emit_l0ffn(env, x_src, x_dst, tok, tag):
    nc = env.nc
    NS = tok // 512
    with ExitStack() as ls:
        xs = env.sbuf(tag + "_xs", [128, 2 * 4096], F32, stack=ls)
        xlnT = env.sbuf(tag + "_xlnT", [128, 8 * 512], BF16, stack=ls)
        hT = env.sbuf(tag + "_hT", [128, 32 * 512], BF16, stack=ls)
        sqT = env.sbuf(tag + "_sqT", [128, 2 * 512], BF16, stack=ls)
        wf = env.sbuf(tag + "_wf", [128, 2 * 4096], BF16, stack=ls)
        wp = env.sbuf(tag + "_wp", [128, 32 * 1024], BF16, stack=ls)
        xln = env.sbuf(tag + "_xln", [128, 4 * 1024], BF16, stack=ls)
        sq = env.sbuf(tag + "_sq", [128, 1024], F32, stack=ls)
        sx = env.sbuf(tag + "_sx", [128, 4], F32, stack=ls)
        sxx = env.sbuf(tag + "_sxx", [128, 4], F32, stack=ls)
        mu4 = env.sbuf(tag + "_mu4", [128, 4], F32, stack=ls)
        var4 = env.sbuf(tag + "_var4", [128, 4], F32, stack=ls)
        rstd4 = env.sbuf(tag + "_rstd4", [128, 4], F32, stack=ls)
        hrstd = env.sbuf(tag + "_hrstd", [128, 4], F32, stack=ls)
        rows = env.sbuf(tag + "_rows", [1, 2048], F32, stack=ls)
        rows_bf = env.sbuf(tag + "_rowsbf", [1, 1024], BF16, stack=ls)
        lnWp = env.sbuf(tag + "_lnWp", [1, 2 * D], BF16, stack=ls)
        ps_m1 = [env.psum(tag + "_pm1a", [128, 512], F32, stack=ls),
                 env.psum(tag + "_pm1b", [128, 512], F32, stack=ls)]
        ps_m2 = [env.psum(tag + "_pm2a", [128, 512], F32, stack=ls),
                 env.psum(tag + "_pm2b", [128, 512], F32, stack=ls)]
        ps_mu = env.psum(tag + "_pmu", [128, 512], F32, stack=ls)
        ps_ssq = env.psum(tag + "_pssq", [128, 512], F32, stack=ls)
        ps_xp = [env.psum(tag + "_pxpa", [128, 1024], BF16, stack=ls),
                 env.psum(tag + "_pxpb", [128, 1024], BF16, stack=ls)]
        if DBGDUMP:
            dbg_xln = env.sbuf(tag + "_dbgxln", [128, 1024], F32, stack=ls)
            dbg_ps = env.sbuf(tag + "_dbgps", [128, 1024], F32, stack=ls)
            dbg_dst = env.sbuf(tag + "_dbgdst", [128, 1024], F32, stack=ls)

        sems = ["ldwp", "vpre", "srstd", "vln", "slnsq", "txpd", "vxpe",
                "tmm1", "sgelu", "ssqq", "tstc", "vrows", "srows", "trct",
                "vcols", "tmm2", "vev"]
        S = {k: env.sem("f_" + k) for k in sems}
        bs = {k: env.base["f_" + k] for k in sems}
        d_ldx = Dma2(env, "f_ldx")
        d_stx = Dma2(env, "f_stx")
        d_ldw = Dma2(env, "f_ldw1")
        d_lnp = Dma2(env, "f_lnp")

        def x3(ap, s):
            return ap[s * 512:(s + 1) * 512, :].rearrange("(a p) d -> p a d", p=128)

        def xlnT_dst(tau):
            full = xlnT[:, :].rearrange("p (c w) -> p c w", c=8)
            return full[:, :, tau * 128:(tau + 1) * 128]

        with nc.Block(tag) as blk:
            @blk.sync
            def _(s):
                d_lnp.start(s, lnWp[0:1, :], env.lnWp_hbm[0:1, :], 0)
                for st in range(NS):
                    if st >= 2:
                        d_stx.wait(s, st - 2)
                    d_ldx.start(s, xs[:, (st % 2) * 4096:(st % 2) * 4096 + 4096]
                                .rearrange("p (a d) -> p a d", a=4),
                                x3(x_src, st), st)

            @blk.gpsimd
            def _(g):
                g = Ser(env, "g", g)
                for j in range(8):
                    g.dma_start(wp[:, j * 4096:(j + 1) * 4096]
                                .rearrange("p (c n) -> p c n", c=4),
                                env.wprojT_s[j * 512:(j + 1) * 512, :]
                                .rearrange("(c p) n -> p c n", p=128)
                                ).then_inc(S["ldwp"], 16)
                for st in range(NS):
                    for grp in range(8):
                        G = st * 8 + grp
                        if G >= 2:
                            g.wait_ge(S["tmm1"], bs["tmm1"] + (G - 1) * 4)
                        d_ldw.start(g, wf[:, (G % 2) * 4096:(G % 2) * 4096 + 4096]
                                    .rearrange("p (c a n) -> p c a n", c=4, a=8),
                                    env.wfcT_q[grp * 4:(grp + 1) * 4]
                                    .rearrange("c a p n -> p c a n"), G)

            @blk.scalar
            def _(sc):
                sc = Ser(env, "s", sc)
                for st in range(NS):
                    xb = xs[:, (st % 2) * 4096:(st % 2) * 4096 + 4096]
                    d_ldx.wait(sc, st)
                    if st >= 1:
                        sc.wait_ge(S["vpre"], bs["vpre"] + st)
                    for tau in range(4):
                        sc.activation(sq[:, :], xb[:, tau * 1024:(tau + 1) * 1024],
                                      AF.Square, accum_out=sxx[:, tau:tau + 1],
                                      inc=(S["slnsq"], 1,
                                           bs["slnsq"] + 4 * st + tau + 1))
                    sc.wait_ge(S["vpre"], bs["vpre"] + st + 1)
                    sc.activation(sq[:, 0:4], var4[:, :], AF.Ln, bias=LN_EPS)
                    sc.activation(rstd4[:, :], sq[:, 0:4], AF.Exp, scale=-0.5,
                                  inc=(S["srstd"], 1, bs["srstd"] + st + 1))
                    for c in range(32):
                        sc.wait_ge(S["tmm1"], bs["tmm1"] + 32 * st + c + 1)
                        if c == 0 and st >= 1:
                            sc.wait_ge(S["tmm2"], bs["tmm2"] + 8 * st)
                        sc.activation(hT[:, c * 512:(c + 1) * 512],
                                      ps_m1[c % 2][:, :], AF.Gelu,
                                      inc=(S["sgelu"], 1,
                                           bs["sgelu"] + 32 * st + c + 1))
                        if 32 * st + c >= 2:
                            sc.wait_ge(S["tstc"], bs["tstc"] + 32 * st + c - 1)
                        sc.activation(sqT[:, (c % 2) * 512:(c % 2) * 512 + 512],
                                      hT[:, c * 512:(c + 1) * 512], AF.Square,
                                      inc=(S["ssqq"], 1,
                                           bs["ssqq"] + 32 * st + c + 1))
                    sc.wait_ge(S["vrows"], bs["vrows"] + st + 1)
                    sc.activation(rows[0:1, 1024:1536], rows[0:1, 512:1024],
                                  AF.Ln, bias=LN_EPS)
                    sc.activation(rows[0:1, 1536:2048], rows[0:1, 1024:1536],
                                  AF.Exp, scale=-0.5)
                    sc.activation(rows_bf[0:1, 512:1024], rows[0:1, 1024:1536],
                                  AF.Exp, scale=0.5,
                                  inc=(S["srows"], 1, bs["srows"] + st + 1))
                    sc.wait_ge(S["vev"], bs["vev"] + 8 * (st + 1))
                    d_stx.start(sc, x3(x_dst, st),
                                xs[:, (st % 2) * 4096:(st % 2) * 4096 + 4096]
                                .rearrange("p (a d) -> p a d", a=4), st)
                d_stx.wait_all(sc)

            @blk.vector
            def _(v):
                v = Ser(env, "v", v)
                for st in range(NS):
                    xb = xs[:, (st % 2) * 4096:(st % 2) * 4096 + 4096]
                    d_ldx.wait(v, st)
                    for tau in range(4):
                        v.reduce_sum(sx[:, tau:tau + 1],
                                     xb[:, tau * 1024:(tau + 1) * 1024], axis=AX.X)
                    v.wait_ge(S["slnsq"], bs["slnsq"] + 4 * (st + 1))
                    v.tensor_scalar(mu4[:, :], sx[:, :], 1.0 / D, None, op0=OP.mult)
                    v.tensor_tensor(sx[:, :], mu4[:, :], mu4[:, :], op=OP.mult)
                    v.tensor_scalar(var4[:, :], sxx[:, :], 1.0 / D, None,
                                    op0=OP.mult)
                    v.tensor_tensor(var4[:, :], var4[:, :], sx[:, :],
                                    op=OP.subtract,
                                    inc=(S["vpre"], 1, bs["vpre"] + st + 1))
                    v.wait_ge(S["srstd"], bs["srstd"] + st + 1)
                    for tau in range(4):
                        if st >= 1:
                            v.wait_ge(S["txpd"], bs["txpd"] + 4 * (st - 1) + tau + 1)
                        v.tensor_scalar(xln[:, tau * 1024:(tau + 1) * 1024],
                                        xb[:, tau * 1024:(tau + 1) * 1024],
                                        mu4[:, tau:tau + 1], rstd4[:, tau:tau + 1],
                                        op0=OP.subtract, op1=OP.mult,
                                        inc=(S["vln"], 1,
                                             bs["vln"] + 4 * st + tau + 1))
                    for tau in range(4):
                        v.wait_ge(S["txpd"], bs["txpd"] + 4 * st + tau + 1)
                        v.tensor_copy(xlnT_dst(tau),
                                      ps_xp[tau % 2][:, :].rearrange(
                                          "p (c w) -> p c w", c=8),
                                      inc=(S["vxpe"], 1,
                                           bs["vxpe"] + 4 * st + tau + 1))
                    v.wait_ge(S["tstc"], bs["tstc"] + 32 * (st + 1))
                    v.tensor_scalar(rows[0:1, 0:512], ps_mu[0:1, 0:512],
                                    1.0 / H_FF, None, op0=OP.mult)
                    v.tensor_tensor(rows[0:1, 1024:1536], rows[0:1, 0:512],
                                    rows[0:1, 0:512], op=OP.mult)
                    v.scalar_tensor_tensor(rows[0:1, 512:1024], ps_ssq[0:1, 0:512],
                                           1.0 / H_FF, rows[0:1, 1024:1536],
                                           op0=OP.mult, op1=OP.subtract)
                    v.tensor_scalar(rows_bf[0:1, 0:512], rows[0:1, 0:512],
                                    -1.0, None, op0=OP.mult,
                                    inc=(S["vrows"], 1, bs["vrows"] + st + 1))
                    v.wait_ge(S["trct"], bs["trct"] + st + 1)
                    v.tensor_scalar(hrstd[:, :], ps_mu[:, 0:4], 0.5, None,
                                    op0=OP.mult,
                                    inc=(S["vcols"], 1, bs["vcols"] + st + 1))
                    for j in range(8):
                        tau, half = divmod(j, 2)
                        v.wait_ge(S["tmm2"], bs["tmm2"] + 8 * st + j + 1)
                        dst = xs[:, (st % 2) * 4096 + tau * 1024 + half * 512:
                                 (st % 2) * 4096 + tau * 1024 + half * 512 + 512]
                        v.scalar_tensor_tensor(dst, ps_m2[j % 2][:, :],
                                               hrstd[:, tau:tau + 1], dst,
                                               op0=OP.mult, op1=OP.add,
                                               inc=(S["vev"], 1,
                                                    bs["vev"] + 8 * st + j + 1))

            @blk.tensor
            def _(t):
                t.wait_ge(S["ldwp"], bs["ldwp"] + 128)
                d_lnp.wait(t, 0)
                for st in range(NS):
                    for tau in range(4):
                        t.wait_ge(S["vln"], bs["vln"] + 4 * st + tau + 1)
                        if 4 * st + tau >= 2:
                            t.wait_ge(S["vxpe"], bs["vxpe"] + 4 * st + tau - 1)
                        ins = None
                        for dc in range(8):
                            ins = t.matmul(ps_xp[tau % 2][:, dc * 128:(dc + 1) * 128],
                                           xln[:, tau * 1024 + dc * 128:
                                               tau * 1024 + dc * 128 + 128],
                                           env.ident_bf[:, :], is_transpose=True,
                                           start=True, stop=True,
                                           skip_group_check=True)
                        ins.then_inc(S["txpd"], 1)
                    t.wait_ge(S["vxpe"], bs["vxpe"] + 4 * (st + 1))
                    for c in range(32):
                        G = st * 8 + c // 4
                        d_ldw.wait(t, G)
                        if 32 * st + c >= 2:
                            t.wait_ge(S["sgelu"], bs["sgelu"] + 32 * st + c - 1)
                        wbuf = wf[:, (G % 2) * 4096:(G % 2) * 4096 + 4096]
                        ins = None
                        for dc in range(8):
                            ins = t.matmul(ps_m1[c % 2][:, :],
                                           wbuf[:, ((c % 4) * 8 + dc) * 128:
                                                ((c % 4) * 8 + dc) * 128 + 128],
                                           xlnT[:, dc * 512:(dc + 1) * 512],
                                           start=(dc == 0), stop=(dc == 7))
                        ins.then_inc(S["tmm1"], 1)
                        if c >= 2:
                            cs = c - 2
                            t.wait_ge(S["ssqq"], bs["ssqq"] + 32 * st + cs + 1)
                            if cs == 0 and st >= 1:
                                t.wait_ge(S["vcols"], bs["vcols"] + st)
                            t.matmul(ps_mu[0:1, 0:512], env.onescol_bf[:, 0:1],
                                     hT[:, cs * 512:(cs + 1) * 512],
                                     start=(cs == 0), stop=False,
                                     skip_group_check=True)
                            t.matmul(ps_ssq[0:1, 0:512], env.onescol_bf[:, 0:1],
                                     sqT[:, (cs % 2) * 512:(cs % 2) * 512 + 512],
                                     start=(cs == 0), stop=False,
                                     skip_group_check=True).then_inc(S["tstc"], 1)
                    for cs in (30, 31):
                        t.wait_ge(S["ssqq"], bs["ssqq"] + 32 * st + cs + 1)
                        t.matmul(ps_mu[0:1, 0:512], env.onescol_bf[:, 0:1],
                                 hT[:, cs * 512:(cs + 1) * 512],
                                 start=False, stop=(cs == 31), skip_group_check=True)
                        t.matmul(ps_ssq[0:1, 0:512], env.onescol_bf[:, 0:1],
                                 sqT[:, (cs % 2) * 512:(cs % 2) * 512 + 512],
                                 start=False, stop=(cs == 31), skip_group_check=True
                                 ).then_inc(S["tstc"], 1)
                    t.wait_ge(S["srows"], bs["srows"] + st + 1)
                    ins = None
                    for tau in range(4):
                        ins = t.matmul(ps_mu[:, tau:tau + 1],
                                       rows[0:1, 1536 + tau * 128:
                                            1536 + tau * 128 + 128],
                                       env.ident_f[0:1, 0:1], is_transpose=True,
                                       start=True, stop=True, skip_group_check=True)
                    ins.then_inc(S["trct"], 1)
                    for j in range(8):
                        tau, half = divmod(j, 2)
                        Gj = 8 * st + j
                        if Gj >= 2:
                            t.wait_ge(S["vev"], bs["vev"] + Gj - 1)
                        if j == 0:
                            t.wait_ge(S["vcols"], bs["vcols"] + st + 1)
                        for c in range(32):
                            t.matmul(ps_m2[j % 2][:, :],
                                     hT[:, c * 512 + tau * 128:
                                        c * 512 + tau * 128 + 128],
                                     wp[:, c * 1024 + half * 512:
                                        c * 1024 + half * 512 + 512],
                                     start=(c == 0), stop=False)
                        t.matmul(ps_m2[j % 2][:, :],
                                 rows_bf[0:1, tau * 128:tau * 128 + 128],
                                 lnWp[0:1, 1024 + half * 512:1024 + half * 512 + 512],
                                 start=False, stop=False)
                        t.matmul(ps_m2[j % 2][:, :],
                                 rows_bf[0:1, 512 + tau * 128:512 + tau * 128 + 128],
                                 lnWp[0:1, half * 512:half * 512 + 512],
                                 start=False, stop=True).then_inc(S["tmm2"], 1)

        d_ldx.done()
        d_stx.done()
        d_ldw.done()
        d_lnp.done()
        env.base["f_ldwp"] += 16 * 8
        env.base["f_vpre"] += NS
        env.base["f_srstd"] += NS
        env.base["f_vln"] += 4 * NS
        env.base["f_slnsq"] += 4 * NS
        env.base["f_txpd"] += 4 * NS
        env.base["f_vxpe"] += 4 * NS
        env.base["f_tmm1"] += 32 * NS
        env.base["f_sgelu"] += 32 * NS
        env.base["f_ssqq"] += 32 * NS
        env.base["f_tstc"] += 32 * NS
        env.base["f_vrows"] += NS
        env.base["f_srows"] += NS
        env.base["f_trct"] += NS
        env.base["f_vcols"] += NS
        env.base["f_tmm2"] += 8 * NS
        env.base["f_vev"] += 8 * NS



def build_l0ffn_program(tok):
    nc = bass.Bass("TRN2", target_bir_lowering=False)
    x_in = nc.dram_tensor("x_in", [tok, D], F32, kind="ExternalInput").ap()
    wfc = nc.dram_tensor("L0_ffn_W_fc", [H_FF, D], F32, kind="ExternalInput").ap()
    lnw = nc.dram_tensor("L0_ffn_ln_w", [H_FF], F32, kind="ExternalInput").ap()
    lnb = nc.dram_tensor("L0_ffn_ln_b", [H_FF], F32, kind="ExternalInput").ap()
    wproj = nc.dram_tensor("L0_ffn_W_proj", [D, H_FF], F32, kind="ExternalInput").ap()
    x_out = nc.dram_tensor("x_out", [tok, D], F32, kind="ExternalOutput").ap()
    with ExitStack() as stack:
        env = Env(nc, stack)
        emit_consts(env)
        emit_l0ffn_prep(env, wfc, lnw, lnb, wproj)
        emit_l0ffn(env, x_in, x_out, tok, "ffn0")
    return nc


def build_l1ffn_program(tok):
    nc = bass.Bass("TRN2", target_bir_lowering=False)
    x_in = nc.dram_tensor("x_in", [tok, D], F32, kind="ExternalInput").ap()
    ws1 = nc.dram_tensor("L1_ffn_ws1", [D, D], F32, kind="ExternalInput").ap()
    bs1 = nc.dram_tensor("L1_ffn_bs1", [D], F32, kind="ExternalInput").ap()
    ws2 = nc.dram_tensor("L1_ffn_ws2", [D, D], F32, kind="ExternalInput").ap()
    bs2 = nc.dram_tensor("L1_ffn_bs2", [D], F32, kind="ExternalInput").ap()
    we1 = nc.dram_tensor("L1_ffn_We1", [E_EXP, D, D], F32, kind="ExternalInput").ap()
    be1 = nc.dram_tensor("L1_ffn_Be1", [E_EXP, D], F32, kind="ExternalInput").ap()
    we2 = nc.dram_tensor("L1_ffn_We2", [E_EXP, D, D], F32, kind="ExternalInput").ap()
    be2 = nc.dram_tensor("L1_ffn_Be2", [E_EXP, D], F32, kind="ExternalInput").ap()
    wr = nc.dram_tensor("L1_ffn_Wr", [D, E_EXP], F32, kind="ExternalInput").ap()
    br = nc.dram_tensor("L1_ffn_br", [E_EXP], F32, kind="ExternalInput").ap()
    x_out = nc.dram_tensor("x_out", [tok, D], F32, kind="ExternalOutput").ap()
    with ExitStack() as stack:
        env = Env(nc, stack)
        emit_consts(env)
        emit_l1ffn_prep(env, we1, we2)
        emit_l1ffn_smalls(env, be1, be2, bs1, bs2, wr, br, stack)
        emit_l1ffn(env, x_in, x_out, tok, "q0", ws1, ws2)
    return nc



E_EXP = 31
K_TOP = 3


def emit_l1ffn_prep(env, we1_ap, we2_ap):
    nc = env.nc
    env.we1_bf = nc.dram_tensor("we1_bf", [E_EXP, 8, 128, 1024], BF16,
                                kind="Internal").ap()
    env.we2_bf = nc.dram_tensor("we2_bf", [E_EXP, 8, 128, 1024], BF16,
                                kind="Internal").ap()

    with ExitStack() as ls:
        wbuf = env.sbuf("l1p_wbuf", [128, 2 * 4096], BF16, stack=ls)
        dl = Dma2(env, "l1p_d")
        do = Dma2(env, "l1p_o")
        with nc.Block("l1p_we") as blk:
            @blk.gpsimd
            def _(g):
                g = Ser(env, "g", g)
                idx = 0
                for mat, (w_ap, wout) in enumerate(
                        [(we1_ap, env.we1_bf), (we2_ap, env.we2_bf)]):
                    for e in range(E_EXP):
                        for q in range(2):
                            if idx >= 2:
                                do.wait(g, idx - 2)
                            dl.start(g, wbuf[:, (idx % 2) * 4096:(idx % 2) * 4096 + 4096]
                                     .rearrange("p (c n) -> p c n", c=4),
                                     w_ap[e, q * 512:(q + 1) * 512, :]
                                     .rearrange("(c p) n -> p c n", p=128), idx)
                            dl.wait(g, idx)
                            do.start(g, wout[e, q * 4:(q + 1) * 4]
                                     .rearrange("c p n -> p c n"),
                                     wbuf[:, (idx % 2) * 4096:(idx % 2) * 4096 + 4096]
                                     .rearrange("p (c n) -> p c n", c=4), idx)
                            idx += 1
                do.wait_all(g)
        dl.done()
        do.done()



def emit_l1ffn_smalls(env, be1_ap, be2_ap, bs1_ap, bs2_ap, wr_ap, br_ap, sstack):
    nc = env.nc
    env.base["smalls_n"] = env.base.get("smalls_n", 0) + 1
    env.bs1_cols = emit_vec_to_cols(
        env, bs1_ap, D, "bs1_" + str(env.base["smalls_n"]))["f32"]
    sfx = "_" + str(env.base["smalls_n"])
    env.be1_cols = env.sbuf("be1_cols" + sfx, [128, 8 * E_EXP], F32, stack=sstack)
    env.be2_bf = env.sbuf("be2_bf" + sfx, [E_EXP, 1024], BF16, stack=sstack)
    env.halfbs2_rep = env.sbuf("halfbs2_rep" + sfx, [128, 1024], F32, stack=sstack)
    env.br_rep = env.sbuf("br_rep" + sfx, [128, E_EXP], F32, stack=sstack)
    env.wr_bf = env.sbuf("wr_bf" + sfx, [128, 8 * E_EXP], BF16, stack=sstack)
    with ExitStack() as ls:
        be1row = env.sbuf("l1p_be1" + sfx, [E_EXP, 1024], F32, stack=ls)
        rowbuf = env.sbuf("l1p_row" + sfx, [1, 1024], F32, stack=ls)
        rowb2 = env.sbuf("l1p_rowb2" + sfx, [1, 1024], F32, stack=ls)
        ps_a = env.psum("l1p_psa" + sfx, [128, 512], F32, stack=ls)
        ps_b = env.psum("l1p_psb" + sfx, [128, E_EXP], F32, stack=ls)
        d = env.sem("l1p_sd")
        gd = env.sem("l1p_gd")
        m = env.sem("l1p_m")
        vm = env.sem("l1p_vm")
        bd = env.base["l1p_sd"]
        bg = env.base["l1p_gd"]
        bm = env.base["l1p_m"]
        bv = env.base["l1p_vm"]
        with nc.Block("l1p_small" + sfx) as blk:
            @blk.sync
            def _(s):
                s.dma_start(be1row[:, :], be1_ap).then_inc(d, 16)
                s.dma_start(rowbuf[:, :], bs2_ap.rearrange("(o n) -> o n", o=1)
                            ).then_inc(d, 16)
                s.dma_start(rowb2[0:1, 0:E_EXP],
                            br_ap.rearrange("(o n) -> o n", o=1)).then_inc(d, 16)
            @blk.gpsimd
            def _(g):
                g = Ser(env, "g", g)
                g.dma_start(env.be2_bf[:, :], be2_ap).then_inc(gd, 16)
                g.dma_start(env.wr_bf[:, :].rearrange("p (c n) -> p c n", c=8),
                            wr_ap.rearrange("(c p) n -> p c n", p=128)
                            ).then_inc(gd, 16)
            @blk.tensor
            def _(t):
                t.wait_ge(d, bd + 48)
                for nch in range(8):
                    t.matmul(ps_a[:, nch * E_EXP:nch * E_EXP + E_EXP],
                             be1row[:, nch * 128:(nch + 1) * 128],
                             env.ident_f[0:E_EXP, 0:E_EXP], is_transpose=True,
                             start=True, stop=True, skip_group_check=True
                             ).then_inc(m, 1)
                t.wait_ge(vm, bv + 1)
                t.matmul(ps_b[:, 0:E_EXP], env.onesrow_f[:, :],
                         rowb2[0:1, 0:E_EXP], start=True, stop=True
                         ).then_inc(m, 1)
                t.wait_ge(vm, bv + 2)
            @blk.vector
            def _(v):
                v = Ser(env, "v", v)
                v.wait_ge(gd, bg + 32)
                v.wait_ge(m, bm + 8)
                v.tensor_copy(env.be1_cols[:, :], ps_a[:, 0:8 * E_EXP])
                v.tensor_scalar(rowbuf[0:1, :], rowbuf[0:1, :], 0.5, None,
                                op0=OP.mult, inc=(vm, 1, bv + 1))
                v.wait_ge(m, bm + 9)
                v.tensor_copy(env.br_rep[:, :], ps_b[:, 0:E_EXP],
                              inc=(vm, 1, bv + 2))
        env.base["l1p_sd"] += 48
        env.base["l1p_gd"] += 32
        env.base["l1p_m"] += 9
        env.base["l1p_vm"] += 2
        with nc.Block("l1p_rep" + sfx) as blk:
            @blk.tensor
            def _(t):
                for q in range(2):
                    t.matmul(ps_a[:, 0:512], env.onesrow_f[:, :],
                             rowbuf[0:1, q * 512:(q + 1) * 512],
                             start=True, stop=True).then_inc(m, 1)
                    t.wait_ge(vm, env.base["l1p_vm"] + q + 1)
            @blk.vector
            def _(v):
                v = Ser(env, "v", v)
                for q in range(2):
                    v.wait_ge(m, env.base["l1p_m"] + q + 1)
                    v.tensor_copy(env.halfbs2_rep[:, q * 512:(q + 1) * 512],
                                  ps_a[:, 0:512], inc=(vm, 1, env.base["l1p_vm"] + q + 1))
        env.base["l1p_m"] += 2
        env.base["l1p_vm"] += 2



DBGSKIP = ()
DBGDUMP = False


def emit_l1ffn(env, x_src, x_dst, tok, tag, ws1_ap, ws2_ap):
    nc = env.nc
    NB = tok // 1024
    NSLOT = E_EXP + 1
    with ExitStack() as ls:
        xs = env.sbuf(tag + "_xs", [128, 8 * 1024], F32, stack=ls)
        acc = env.sbuf(tag + "_acc", [128, 8 * 1024], F32, stack=ls)
        xlnT = env.sbuf(tag + "_xlnT", [128, 8 * 1024], BF16, stack=ls)
        xln = env.sbuf(tag + "_xln", [128, 8 * 1024], BF16, stack=ls)
        heT = env.sbuf(tag + "_heT", [128, 8 * 1024], BF16, stack=ls)
        w1b = env.sbuf(tag + "_w1b", [128, 2 * 8192], BF16, stack=ls)
        w2b = env.sbuf(tag + "_w2b", [128, 2 * 8192], BF16, stack=ls)
        sq = env.sbuf(tag + "_sq", [128, 1024], F32, stack=ls)
        sx = env.sbuf(tag + "_sx", [128, 8], F32, stack=ls)
        sxx = env.sbuf(tag + "_sxx", [128, 8], F32, stack=ls)
        mu8 = env.sbuf(tag + "_mu8", [128, 8], F32, stack=ls)
        var8 = env.sbuf(tag + "_var8", [128, 8], F32, stack=ls)
        rstd8 = env.sbuf(tag + "_rstd8", [128, 8], F32, stack=ls)
        rt = env.sbuf(tag + "_rt", [128, E_EXP], F32, stack=ls)
        rt2 = env.sbuf(tag + "_rt2", [128, E_EXP], F32, stack=ls)
        mx8 = env.sbuf(tag + "_mx8", [128, 8], F32, stack=ls)
        ellc = env.sbuf(tag + "_ell", [128, 3], F32, stack=ls)
        gates = env.sbuf(tag + "_gates", [128, 8 * E_EXP], F32, stack=ls)
        gT = env.sbuf(tag + "_gT", [E_EXP, 8 * 128], BF16, stack=ls)
        ps_m1 = [env.psum(tag + "_pm1a", [128, 512], F32, stack=ls),
                 env.psum(tag + "_pm1b", [128, 512], F32, stack=ls)]
        ps_m2 = [env.psum(tag + "_pm2a", [128, 512], F32, stack=ls),
                 env.psum(tag + "_pm2b", [128, 512], F32, stack=ls)]
        ps_rt = env.psum(tag + "_prt", [128, E_EXP], F32, stack=ls)
        ps_gt = env.psum(tag + "_pgt", [128, 512], F32, stack=ls)
        ps_xp = [env.psum(tag + "_pxpa", [128, 1024], BF16, stack=ls),
                 env.psum(tag + "_pxpb", [128, 1024], BF16, stack=ls)]
        if DBGDUMP:
            dbg_xln = env.sbuf(tag + "_dbgxln", [128, 1024], F32, stack=ls)
            dbg_ps = env.sbuf(tag + "_dbgps", [128, 1024], F32, stack=ls)
            dbg_dst = env.sbuf(tag + "_dbgdst", [128, 1024], F32, stack=ls)

        sems = ["vpre", "srstd", "vln", "slnsq", "txpd", "vxpe",
                "trt", "vnm", "srexp", "vgate", "tgt", "vgte",
                "tm1", "sg", "tm2", "vev", "vfin", "ldsh"]
        S = {k: env.sem("q_" + k) for k in sems}
        bs = {k: env.base["q_" + k] for k in sems}
        d_ldx = Dma2(env, "q_ldx")
        d_stx = Dma2(env, "q_stx")
        d_w1 = Dma2(env, "q_w1")
        d_w2 = Dma2(env, "q_w2")

        def x3(ap, b_):
            return ap[b_ * 1024:(b_ + 1) * 1024, :].rearrange(
                "(a p) d -> p a d", p=128)

        def xlnT_dst(tau):
            full = xlnT[:, :].rearrange("p (c w) -> p c w", c=8)
            return full[:, :, tau * 128:(tau + 1) * 128]

        ST2 = NSLOT * 16 + 16
        with nc.Block(tag) as blk:
            @blk.sync
            def _(s):
                for b_ in range(NB):
                    if b_ >= 1:
                        d_stx.wait(s, b_ - 1)
                    d_ldx.start(s, xs[:, :].rearrange("p (a d) -> p a d", a=8),
                                x3(x_src, b_), b_)

            @blk.gpsimd
            def _(g):
                g = Ser(env, "g", g)
                for b_ in range(NB):
                    for sl in range(NSLOT):
                        gi = b_ * NSLOT + sl
                        if gi >= 2:
                            b2, s2 = divmod(gi - 2, NSLOT)
                            g.wait_ge(S["tm2"], bs["tm2"] + b2 * ST2 + (s2 + 1) * 16)
                        if sl == 0:
                            d_w1.start(g, w1b[:, (gi % 2) * 8192:(gi % 2) * 8192 + 8192]
                                       .rearrange("p (c n) -> p c n", c=8),
                                       ws1_ap.rearrange("(c p) n -> p c n", p=128),
                                       gi)
                            d_w2.start(g, w2b[:, (gi % 2) * 8192:(gi % 2) * 8192 + 8192]
                                       .rearrange("p (c n) -> p c n", c=8),
                                       ws2_ap.rearrange("(c p) n -> p c n", p=128),
                                       gi)
                        else:
                            e = sl - 1
                            d_w1.start(g, w1b[:, (gi % 2) * 8192:(gi % 2) * 8192 + 8192]
                                       .rearrange("p (c n) -> p c n", c=8),
                                       env.we1_bf[e].rearrange("c p n -> p c n"),
                                       gi)
                            d_w2.start(g, w2b[:, (gi % 2) * 8192:(gi % 2) * 8192 + 8192]
                                       .rearrange("p (c n) -> p c n", c=8),
                                       env.we2_bf[e].rearrange("c p n -> p c n"),
                                       gi)

            @blk.scalar
            def _(sc):
                sc = Ser(env, "s", sc)
                for b_ in range(NB):
                    d_ldx.wait(sc, b_)
                    if b_ >= 1:
                        sc.wait_ge(S["vpre"], bs["vpre"] + b_)
                    for tau in range(8):
                        sc.activation(sq[:, :], xs[:, tau * 1024:(tau + 1) * 1024],
                                      AF.Square, accum_out=sxx[:, tau:tau + 1],
                                      inc=(S["slnsq"], 1, bs["slnsq"] + 8 * b_ + tau + 1))
                    sc.wait_ge(S["vpre"], bs["vpre"] + b_ + 1)
                    sc.activation(sq[:, 0:8], var8[:, :], AF.Ln, bias=LN_EPS)
                    sc.activation(rstd8[:, :], sq[:, 0:8], AF.Exp, scale=-0.5,
                                  inc=(S["srstd"], 1, bs["srstd"] + b_ + 1))
                    for tau in range(8):
                        sc.wait_ge(S["vnm"], bs["vnm"] + 8 * b_ + tau + 1)
                        sc.activation(rt2[:, :], rt[:, :], AF.Exp,
                                      bias=ellc[:, 0:1], accum_out=ellc[:, 1:2],
                                      inc=(S["srexp"], 1, bs["srexp"] + 8 * b_ + tau + 1))
                    for sl in range(NSLOT):
                        gi = b_ * NSLOT + sl
                        for u in range(16):
                            stp, nch = divmod(u, 8)
                            sc.wait_ge(S["tm1"], bs["tm1"] + gi * 16 + u + 1)
                            if sl == 0:
                                bias = env.bs1_cols[:, nch:nch + 1]
                            else:
                                bias = env.be1_cols[:, nch * E_EXP + sl - 1:
                                                    nch * E_EXP + sl]
                            sc.activation(heT[:, nch * 1024 + stp * 512:
                                              nch * 1024 + stp * 512 + 512],
                                          ps_m1[u % 2][:, :], AF.Gelu, bias=bias,
                                          inc=(S["sg"], 1, bs["sg"] + gi * 16 + u + 1))
                    sc.wait_ge(S["vfin"], bs["vfin"] + 8 * (b_ + 1))
                    d_stx.start(sc, x3(x_dst, b_),
                                xs[:, :].rearrange("p (a d) -> p a d", a=8), b_)
                d_stx.wait_all(sc)

            @blk.vector
            def _(v):
                v = Ser(env, "v", v)
                for b_ in range(NB):
                    d_ldx.wait(v, b_)
                    for tau in range(8):
                        v.reduce_sum(sx[:, tau:tau + 1],
                                     xs[:, tau * 1024:(tau + 1) * 1024], axis=AX.X)
                    v.wait_ge(S["slnsq"], bs["slnsq"] + 8 * (b_ + 1))
                    v.tensor_scalar(mu8[:, :], sx[:, :], 1.0 / D, None, op0=OP.mult)
                    v.tensor_tensor(sx[:, :], mu8[:, :], mu8[:, :], op=OP.mult)
                    v.tensor_scalar(var8[:, :], sxx[:, :], 1.0 / D, None,
                                    op0=OP.mult)
                    v.tensor_tensor(var8[:, :], var8[:, :], sx[:, :],
                                    op=OP.subtract,
                                    inc=(S["vpre"], 1, bs["vpre"] + b_ + 1))
                    v.wait_ge(S["srstd"], bs["srstd"] + b_ + 1)
                    for tau in range(8):
                        if b_ >= 1:
                            v.wait_ge(S["txpd"], bs["txpd"] + 8 * (b_ - 1) + tau + 1)
                        v.tensor_scalar(xln[:, tau * 1024:(tau + 1) * 1024],
                                        xs[:, tau * 1024:(tau + 1) * 1024],
                                        mu8[:, tau:tau + 1], rstd8[:, tau:tau + 1],
                                        op0=OP.subtract, op1=OP.mult,
                                        inc=(S["vln"], 1, bs["vln"] + 8 * b_ + tau + 1))
                    for tau in range(8):
                        v.wait_ge(S["txpd"], bs["txpd"] + 8 * b_ + tau + 1)
                        v.tensor_copy(xlnT_dst(tau),
                                      ps_xp[tau % 2][:, :].rearrange(
                                          "p (c w) -> p c w", c=8),
                                      inc=(S["vxpe"], 1, bs["vxpe"] + 8 * b_ + tau + 1))
                    for tau in range(8):
                        v.wait_ge(S["trt"], bs["trt"] + 8 * b_ + tau + 1)
                        v.tensor_tensor(rt[:, :], ps_rt[:, :], env.br_rep[:, :],
                                        op=OP.add)
                        v.reduce_max(ellc[:, 0:1], rt[:, :], axis=AX.X, negate=True,
                                     inc=(S["vnm"], 1, bs["vnm"] + 8 * b_ + tau + 1))
                        v.wait_ge(S["srexp"], bs["srexp"] + 8 * b_ + tau + 1)
                        v.reciprocal(ellc[:, 2:3], ellc[:, 1:2])
                        v.tensor_scalar(ellc[:, 2:3], ellc[:, 2:3], 0.5, None,
                                        op0=OP.mult)
                        v.tensor_scalar(rt2[:, :], rt2[:, :], ellc[:, 2:3], None,
                                        op0=OP.mult)
                        v.max(mx8[:, :], rt2[:, :])
                        v.scalar_tensor_tensor(gates[:, tau * E_EXP:(tau + 1) * E_EXP],
                                               rt2[:, :], mx8[:, 2:3], rt2[:, :],
                                               op0=OP.is_ge, op1=OP.mult,
                                               inc=(S["vgate"], 1, bs["vgate"] + 8 * b_ + tau + 1))
                        v.wait_ge(S["tgt"], bs["tgt"] + 8 * b_ + tau + 1)
                        v.tensor_copy(gT[:, tau * 128:(tau + 1) * 128],
                                      ps_gt[0:E_EXP, (tau % 4) * 128:(tau % 4) * 128 + 128],
                                      inc=(S["vgte"], 1, bs["vgte"] + 8 * b_ + tau + 1))
                    for sl in range(NSLOT):
                        for u in range(16):
                            tau, half = divmod(u, 2)
                            ju = b_ * ST2 + sl * 16 + u
                            v.wait_ge(S["tm2"], bs["tm2"] + ju + 1)
                            dst = acc[:, tau * 1024 + half * 512:
                                      tau * 1024 + half * 512 + 512]
                            if sl == 0:
                                sh_mult = 0.0 if "shared" in DBGSKIP else 0.5
                                v.scalar_tensor_tensor(
                                    dst, ps_m2[ju % 2][:, :], sh_mult,
                                    env.halfbs2_rep[:, half * 512:half * 512 + 512],
                                    op0=OP.mult, op1=OP.add)
                                if DBGDUMP and b_ == 0 and tau == 0:
                                    v.tensor_copy(dbg_dst[:, half * 512:
                                                  half * 512 + 512], dst)
                                xl_mult = 0.0 if "xln" in DBGSKIP else 0.5
                                v.scalar_tensor_tensor(
                                    dst, xln[:, tau * 1024 + half * 512:
                                             tau * 1024 + half * 512 + 512],
                                    xl_mult, dst, op0=OP.mult, op1=OP.add,
                                    inc=(S["vev"], 1, bs["vev"] + ju + 1))
                                if DBGDUMP and b_ == 0 and tau == 0:
                                    v.tensor_copy(dbg_xln[:, half * 512:
                                                  half * 512 + 512],
                                                  dst)
                            elif "experts" in DBGSKIP:
                                v.scalar_tensor_tensor(
                                    dst, ps_m2[ju % 2][:, :], 0.0,
                                    dst, op0=OP.mult, op1=OP.add,
                                    inc=(S["vev"], 1, bs["vev"] + ju + 1))
                            else:
                                v.scalar_tensor_tensor(
                                    dst, ps_m2[ju % 2][:, :],
                                    gates[:, tau * E_EXP + sl - 1:tau * E_EXP + sl],
                                    dst, op0=OP.mult, op1=OP.add,
                                    inc=(S["vev"], 1, bs["vev"] + ju + 1))
                                if (DBGDUMP and b_ == 0 and tau == 0
                                        and sl == NSLOT - 1):
                                    v.tensor_copy(dbg_ps[:, half * 512:
                                                  half * 512 + 512], dst)
                    for u in range(16):
                        tau, half = divmod(u, 2)
                        ju = b_ * ST2 + NSLOT * 16 + u
                        v.wait_ge(S["tm2"], bs["tm2"] + ju + 1)
                        dst = acc[:, tau * 1024 + half * 512:
                                  tau * 1024 + half * 512 + 512]
                        if "be2" in DBGSKIP:
                            v.scalar_tensor_tensor(
                                dst, ps_m2[ju % 2][:, :], 0.0, dst,
                                op0=OP.mult, op1=OP.add,
                                inc=(S["vev"], 1, bs["vev"] + ju + 1))
                        else:
                            v.tensor_tensor(dst, dst, ps_m2[ju % 2][:, :], op=OP.add,
                                            inc=(S["vev"], 1, bs["vev"] + ju + 1))
                    for tau in range(8):
                        xt = xs[:, tau * 1024:(tau + 1) * 1024]
                        v.tensor_tensor(xt, xt,
                                        acc[:, tau * 1024:(tau + 1) * 1024],
                                        op=OP.add,
                                        inc=(S["vfin"], 1,
                                             bs["vfin"] + 8 * b_ + tau + 1))

            @blk.tensor
            def _(t):
                for b_ in range(NB):
                    for tau in range(8):
                        t.wait_ge(S["vln"], bs["vln"] + 8 * b_ + tau + 1)
                        if 8 * b_ + tau >= 2:
                            t.wait_ge(S["vxpe"], bs["vxpe"] + 8 * b_ + tau - 1)
                        ins = None
                        for dc in range(8):
                            ins = t.matmul(ps_xp[tau % 2][:, dc * 128:(dc + 1) * 128],
                                           xln[:, tau * 1024 + dc * 128:
                                               tau * 1024 + dc * 128 + 128],
                                           env.ident_bf[:, :], is_transpose=True,
                                           start=True, stop=True,
                                           skip_group_check=True)
                        ins.then_inc(S["txpd"], 1)
                    t.wait_ge(S["vxpe"], bs["vxpe"] + 8 * (b_ + 1))
                    for tau in range(8):
                        if 8 * b_ + tau >= 1:
                            t.wait_ge(S["vnm"], bs["vnm"] + 8 * b_ + tau)
                        ins = None
                        for dc in range(8):
                            ins = t.matmul(ps_rt[:, :],
                                           xlnT[:, dc * 1024 + tau * 128:
                                                dc * 1024 + tau * 128 + 128],
                                           env.wr_bf[:, dc * E_EXP:(dc + 1) * E_EXP],
                                           start=(dc == 0), stop=(dc == 7))
                        ins.then_inc(S["trt"], 1)
                        t.wait_ge(S["vgate"], bs["vgate"] + 8 * b_ + tau + 1)
                        if 8 * b_ + tau >= 4:
                            t.wait_ge(S["vgte"], bs["vgte"] + 8 * b_ + tau - 3)
                        t.matmul(ps_gt[0:E_EXP, (tau % 4) * 128:(tau % 4) * 128 + 128],
                                 gates[:, tau * E_EXP:(tau + 1) * E_EXP],
                                 env.ident_f[0:128, 0:128], is_transpose=True,
                                 start=True, stop=True, skip_group_check=True
                                 ).then_inc(S["tgt"], 1)
                    for sl in range(NSLOT):
                        gi = b_ * NSLOT + sl
                        w1 = w1b[:, (gi % 2) * 8192:(gi % 2) * 8192 + 8192]
                        w2 = w2b[:, (gi % 2) * 8192:(gi % 2) * 8192 + 8192]
                        d_w1.wait(t, gi)
                        for u in range(16):
                            stp, nch = divmod(u, 8)
                            if gi * 16 + u >= 2:
                                t.wait_ge(S["sg"], bs["sg"] + gi * 16 + u - 1)
                            ins = None
                            for dc in range(8):
                                ins = t.matmul(ps_m1[u % 2][:, :],
                                               w1[:, dc * 1024 + nch * 128:
                                                  dc * 1024 + nch * 128 + 128],
                                               xlnT[:, dc * 1024 + stp * 512:
                                                    dc * 1024 + stp * 512 + 512],
                                               start=(dc == 0), stop=(dc == 7))
                            ins.then_inc(S["tm1"], 1)
                        d_w2.wait(t, gi)
                        t.wait_ge(S["sg"], bs["sg"] + (gi + 1) * 16)
                        if sl == 1:
                            t.wait_ge(S["vgate"], bs["vgate"] + 8 * (b_ + 1))
                        for u in range(16):
                            tau, half = divmod(u, 2)
                            ju = b_ * ST2 + sl * 16 + u
                            if ju >= 2:
                                t.wait_ge(S["vev"], bs["vev"] + ju - 1)
                            ins = None
                            for nch in range(8):
                                ins = t.matmul(ps_m2[ju % 2][:, :],
                                               heT[:, nch * 1024 + tau * 128:
                                                   nch * 1024 + tau * 128 + 128],
                                               w2[:, nch * 1024 + half * 512:
                                                  nch * 1024 + half * 512 + 512],
                                               start=(nch == 0), stop=(nch == 7))
                            ins.then_inc(S["tm2"], 1)
                    t.wait_ge(S["vgte"], bs["vgte"] + 8 * (b_ + 1))
                    for u in range(16):
                        tau, half = divmod(u, 2)
                        ju = b_ * ST2 + NSLOT * 16 + u
                        if ju >= 2:
                            t.wait_ge(S["vev"], bs["vev"] + ju - 1)
                        t.matmul(ps_m2[ju % 2][:, :],
                                 gT[:, tau * 128:(tau + 1) * 128],
                                 env.be2_bf[:, half * 512:half * 512 + 512],
                                 start=True, stop=True).then_inc(S["tm2"], 1)

        d_ldx.done()
        d_stx.done()
        d_w1.done()
        d_w2.done()
        env.base["q_vpre"] += NB
        env.base["q_srstd"] += NB
        env.base["q_vln"] += 8 * NB
        env.base["q_slnsq"] += 8 * NB
        env.base["q_txpd"] += 8 * NB
        env.base["q_vxpe"] += 8 * NB
        env.base["q_trt"] += 8 * NB
        env.base["q_vnm"] += 8 * NB
        env.base["q_srexp"] += 8 * NB
        env.base["q_vgate"] += 8 * NB
        env.base["q_tgt"] += 8 * NB
        env.base["q_vgte"] += 8 * NB
        env.base["q_tm1"] += NSLOT * 16 * NB
        env.base["q_sg"] += NSLOT * 16 * NB
        env.base["q_tm2"] += ST2 * NB
        env.base["q_vev"] += ST2 * NB
        env.base["q_vfin"] += 8 * NB



def emit_wT_f32(env, w_ap, out_q_ap, tag):
    nc = env.nc
    with ExitStack() as ls:
        wsb = env.sbuf(tag + "_wsb", [128, 8 * 1024], F32, stack=ls)
        stg = env.sbuf(tag + "_stg", [128, 2 * 1024], F32R, stack=ls)
        ps = [env.psum(tag + "_psa", [128, 512], F32, stack=ls),
              env.psum(tag + "_psb", [128, 512], F32, stack=ls)]
        ts = env.sem("wtf_t")
        vs = env.sem("wtf_v")
        dl = Dma2(env, "wtf_d")
        do = Dma2(env, "wtf_o")
        bt = env.base["wtf_t"]
        bv = env.base["wtf_v"]
        with nc.Block(tag) as blk:
            @blk.sync
            def _(s):
                dl.start(s, wsb[:, :].rearrange("p (c n) -> p c n", c=8),
                         w_ap.rearrange("(c p) n -> p c n", p=128), 0)
            @blk.tensor
            def _(t):
                dl.wait(t, 0)
                for kt in range(8):
                    for hb in range(2):
                        gs = kt * 2 + hb
                        if gs >= 2:
                            t.wait_ge(vs, bv + gs - 1)
                        ins = None
                        for q in range(4):
                            dc = hb * 4 + q
                            ins = t.matmul(ps[hb][:, q * 128:(q + 1) * 128],
                                           wsb[:, kt * 1024 + dc * 128:
                                               kt * 1024 + dc * 128 + 128],
                                           env.ident_f[:, :], is_transpose=True,
                                           start=True, stop=True,
                                           skip_group_check=True)
                        ins.then_inc(ts, 1)
            @blk.vector
            def _(v):
                v = Ser(env, "v", v)
                for kt in range(8):
                    for hb in range(2):
                        gs = kt * 2 + hb
                        v.wait_ge(ts, bt + gs + 1)
                        if kt >= 2 and hb == 0:
                            do.wait(v, kt - 2)
                        v.tensor_copy(stg[:, (kt % 2) * 1024 + hb * 512:
                                          (kt % 2) * 1024 + hb * 512 + 512],
                                      ps[hb][:, :], inc=(vs, 1, bv + gs + 1))
            @blk.gpsimd
            def _(g):
                g = Ser(env, "g", g)
                for kt in range(8):
                    g.wait_ge(vs, bv + kt * 2 + 2)
                    do.start(g, out_q_ap[kt].rearrange("c p n -> p c n"),
                             stg[:, (kt % 2) * 1024:(kt % 2) * 1024 + 1024]
                             .rearrange("p (c n) -> p c n", c=8), kt)
                do.wait_all(g)
        dl.done()
        do.done()
        env.base["wtf_t"] += 16
        env.base["wtf_v"] += 16


def emit_fold(env, wfirst_ap, wsecond_ap, tag, dst_fn):
    nc = env.nc
    with ExitStack() as ls:
        wtmp = env.sbuf(tag + "_wtmp", [128, 8 * 1024], F32, stack=ls)
        wtmp2 = env.sbuf(tag + "_wtmp2", [128, 8 * 1024], F32R, stack=ls)
        w2T = env.sbuf(tag + "_w2T", [128, 8 * 1024], F32R, stack=ls)
        stg = env.sbuf(tag + "_stg", [128, 2 * 512], F32R, stack=ls)
        ps = [env.psum(tag + "_psa", [128, 512], F32, stack=ls),
              env.psum(tag + "_psb", [128, 512], F32, stack=ls)]
        ts = env.sem("fld_t")
        vs = env.sem("fld_v")
        dl = Dma2(env, "fld_d")
        do = Dma2(env, "fld_o")
        bt = env.base["fld_t"]
        bv = env.base["fld_v"]
        with nc.Block(tag + "_a") as blk:
            @blk.sync
            def _(s):
                dl.start(s, wtmp[:, :].rearrange("p (c n) -> p c n", c=8),
                         wsecond_ap.rearrange("(c p) n -> p c n", p=128), 0)
            @blk.tensor
            def _(t):
                dl.wait(t, 0)
                for kt in range(8):
                    for hb in range(2):
                        gs = kt * 2 + hb
                        if gs >= 2:
                            t.wait_ge(vs, bv + gs - 1)
                        ins = None
                        for q in range(4):
                            dc = hb * 4 + q
                            ins = t.matmul(ps[hb][:, q * 128:(q + 1) * 128],
                                           wtmp[:, kt * 1024 + dc * 128:
                                               kt * 1024 + dc * 128 + 128],
                                           env.ident_f[:, :], is_transpose=True,
                                           start=True, stop=True,
                                           skip_group_check=True)
                        ins.then_inc(ts, 1)
            @blk.vector
            def _(v):
                v = Ser(env, "v", v)
                for kt in range(8):
                    for hb in range(2):
                        gs = kt * 2 + hb
                        v.wait_ge(ts, bt + gs + 1)
                        v.tensor_copy(
                            w2T[:, :].rearrange("p (c n) -> p c n", c=8)
                            [:, hb * 4:(hb + 1) * 4, kt * 128:(kt + 1) * 128],
                            ps[hb][:, :].rearrange("p (c n) -> p c n", c=4),
                            inc=(vs, 1, bv + gs + 1))
        env.base["fld_t"] += 16
        env.base["fld_v"] += 16
        bt = env.base["fld_t"]
        bv = env.base["fld_v"]
        with nc.Block(tag + "_b") as blk:
            @blk.sync
            def _(s):
                s.wait_ge(vs, bv)
                dl.start(s, wtmp[:, :].rearrange("p (c n) -> p c n", c=8),
                         wfirst_ap.rearrange("(c p) n -> p c n", p=128), 1)
            @blk.tensor
            def _(t):
                t.wait_ge(vs, bv + 1)
                for dcc in range(8):
                    for qs in range(2):
                        gs = dcc * 2 + qs
                        if gs >= 2:
                            t.wait_ge(vs, bv + gs)
                        ins = None
                        for kc in range(8):
                            ins = t.matmul(
                                ps[gs % 2][:, :],
                                wtmp2[:, kc * 1024 + dcc * 128:
                                      kc * 1024 + dcc * 128 + 128],
                                w2T[:, kc * 1024 + qs * 512:
                                    kc * 1024 + qs * 512 + 512],
                                start=(kc == 0), stop=(kc == 7))
                        ins.then_inc(ts, 1)
            @blk.vector
            def _(v):
                v = Ser(env, "v", v)
                dl.wait(v, 1)
                v.tensor_copy(wtmp2[:, :], wtmp[:, :], inc=(vs, 1, bv + 1))
                for gs in range(16):
                    v.wait_ge(ts, bt + gs + 1)
                    if gs >= 2:
                        do.wait(v, gs - 2)
                    v.tensor_copy(stg[:, (gs % 2) * 512:(gs % 2) * 512 + 512],
                                  ps[gs % 2][:, :], inc=(vs, 1, bv + gs + 2))
            @blk.gpsimd
            def _(g):
                g = Ser(env, "g", g)
                for dcc in range(8):
                    for qs in range(2):
                        gs = dcc * 2 + qs
                        g.wait_ge(vs, bv + gs + 2)
                        do.start(g, dst_fn(dcc, qs),
                                 stg[:, (gs % 2) * 512:(gs % 2) * 512 + 512], gs)
                do.wait_all(g)
        dl.done()
        do.done()
        env.base["fld_t"] += 16
        env.base["fld_v"] += 17


def emit_l1attn_prep(env, wk_ap, wq_ap, wv_ap, wa_ap, wb_ap):
    nc = env.nc
    env.wqT_q = nc.dram_tensor("wqT_q", [8, 8, 128, 128], F32R, kind="Internal").ap()
    env.wkaT_q = nc.dram_tensor("wkaT_q", [8, 8, 128, 128], F32R, kind="Internal").ap()
    env.wvbT_f = nc.dram_tensor("wvbT_f", [1024, 1024], F32R, kind="Internal").ap()
    emit_wT_f32(env, wq_ap, env.wqT_q, "pwq")

    def ka_dst(dcc, qs):
        return env.wkaT_q[qs * 4:(qs + 1) * 4, dcc].rearrange("c p n -> p c n")
    emit_fold(env, wk_ap, wa_ap, "pka", ka_dst)

    def vb_dst(dcc, qs):
        return env.wvbT_f[dcc * 128:(dcc + 1) * 128, qs * 512:(qs + 1) * 512]
    emit_fold(env, wv_ap, wb_ap, "pvb", vb_dst)



def emit_l1attn(env, x_src, x_dst, nb, tag):
    nc = env.nc
    qT_hbm = nc.dram_tensor(tag + "_qT", [16, 8, 128, 128], F32R, kind="Internal").ap()
    with ExitStack() as ls:
        xs = env.sbuf(tag + "_xs", [128, 2 * 1024], F32, stack=ls)
        xr = env.sbuf(tag + "_xr", [128, 2 * 1024], F32, stack=ls)
        xln = env.sbuf(tag + "_xln", [128, 1024], F32, stack=ls)
        xlnT = env.sbuf(tag + "_xlnT", [128, 8 * 2048], F32R, stack=ls)
        akT = env.sbuf(tag + "_akT", [128, 8 * 2048], F32R, stack=ls)
        u_bf = env.sbuf(tag + "_u", [128, 16 * 1024], BF16, stack=ls)
        wbuf = env.sbuf(tag + "_wbuf", [128, 2 * 1024], F32R, stack=ls)
        qstg = env.sbuf(tag + "_qstg", [128, 2 * 512], F32R, stack=ls)
        sx = env.sbuf(tag + "_sx", [128, 16], F32, stack=ls)
        sxx = env.sbuf(tag + "_sxx", [128, 16], F32, stack=ls)
        mu16 = env.sbuf(tag + "_mu", [128, 16], F32, stack=ls)
        var16 = env.sbuf(tag + "_var", [128, 16], F32, stack=ls)
        rstd16 = env.sbuf(tag + "_rstd", [128, 16], F32, stack=ls)
        sq = env.sbuf(tag + "_sq", [128, 16], F32, stack=ls)
        nmx = env.sbuf(tag + "_nmx", [128, 3], F32, stack=ls)
        wvb_v = akT[:, 0:4096]
        qt_v = xlnT[:, 0:2048]
        pn = env.sbuf(tag + "_pn", [128, 2048], BF16, stack=ls)
        pt = env.sbuf(tag + "_pt", [128, 2048], BF16, stack=ls)
        pn_v = pn[:, :]
        pt_v = pt[:, :]
        ps_e = env.psum(tag + "_pe", [128, 2048], F32, stack=ls)
        ps_pt = [env.psum(tag + "_ppta", [128, 1024], BF16, stack=ls),
                 env.psum(tag + "_pptb", [128, 1024], BF16, stack=ls)]
        ps_av = [env.psum(tag + "_pava", [128, 512], F32, stack=ls),
                 env.psum(tag + "_pavb", [128, 512], F32, stack=ls)]

        sems = ["vpre", "srstd", "vln", "slnsq", "txpd", "vxpe",
                "tpj", "spje", "te", "vnm", "spx", "tpt", "vpte", "tav", "vfin",
                "ldvb"]
        S = {k: env.sem("a_" + k) for k in sems}
        bs = {k: env.base["a_" + k] for k in sems}
        d_lx = Dma2(env, "a_lx")
        d_lo = Dma2(env, "a_lo")
        d_ql = Dma2(env, "a_ql")
        d_st = Dma2(env, "a_st")
        d_wt = Dma2(env, "a_wt")
        d_qo = Dma2(env, "a_qo")

        def xtile(ap, b_, i):
            return ap[b_ * T + i * 128:b_ * T + i * 128 + 128, :]

        with nc.Block(tag) as blk:
            @blk.sync
            def _(s):
                for b_ in range(nb):
                    for i in range(16):
                        gi = b_ * 16 + i
                        if gi >= 2:
                            s.wait_ge(S["vln"], bs["vln"] + gi - 1)
                        d_lx.start(s, xs[:, (gi % 2) * 1024:(gi % 2) * 1024 + 1024],
                                   xtile(x_src, b_, i), gi)
                    d_qo.wait(s, b_ * 32 + 30)
                    d_qo.wait(s, b_ * 32 + 31)
                    for i in range(16):
                        gi = b_ * 16 + i
                        if gi >= 2:
                            s.wait_ge(S["te"], bs["te"] + 4 * (gi - 1))
                        d_ql.start(s, qt_v[:, (gi % 2) * 1024:(gi % 2) * 1024 + 1024]
                                   .rearrange("p (c n) -> p c n", c=8),
                                   qT_hbm[i].rearrange("c p n -> p c n"), gi)
                        if gi >= 2:
                            d_st.wait(s, gi - 2)
                        d_lo.start(s, xr[:, (gi % 2) * 1024:(gi % 2) * 1024 + 1024],
                                   xtile(x_src, b_, i), gi)

            @blk.gpsimd
            def _(g):
                g = Ser(env, "g", g)
                for b_ in range(nb):
                    for half in range(2):
                        if b_ >= 1:
                            g.wait_ge(S["te"], bs["te"] + 64 * b_)
                        if half == 1:
                            g.wait_ge(S["tpj"], bs["tpj"] + b_ * 96 + 16)
                        g.dma_start(wvb_v[:, :].rearrange("p (c n) -> p c n", c=8),
                                    env.wvbT_f[:, half * 512:(half + 1) * 512]
                                    .rearrange("(c p) n -> p c n", p=128)
                                    ).then_inc(S["ldvb"], 16)
                    for m in range(2):
                        wsrc = [env.wkaT_q, env.wqT_q][m]
                        for oc in range(8):
                            j = m * 8 + oc
                            gi = b_ * 16 + j
                            if j >= 2:
                                g.wait_ge(S["tpj"],
                                          bs["tpj"] + b_ * 96 + 32 + (j - 1) * 4)
                            elif b_ >= 1:
                                g.wait_ge(S["tpj"], bs["tpj"] + b_ * 96)
                            d_wt.start(g, wbuf[:, (gi % 2) * 1024:
                                               (gi % 2) * 1024 + 1024]
                                       .rearrange("p (c n) -> p c n", c=8),
                                       wsrc[oc].rearrange("c p n -> p c n"), gi)

            @blk.scalar
            def _(sc):
                sc = Ser(env, "s", sc)
                for b_ in range(nb):
                    for i in range(16):
                        gi = b_ * 16 + i
                        d_lx.wait(sc, gi)
                        if gi >= 1:
                            sc.wait_ge(S["txpd"], bs["txpd"] + 2 * gi)
                        sc.activation(xln[:, :],
                                      xs[:, (gi % 2) * 1024:(gi % 2) * 1024 + 1024],
                                      AF.Square, accum_out=sxx[:, i:i + 1],
                                      inc=(S["slnsq"], 1, bs["slnsq"] + gi + 1))
                        sc.wait_ge(S["vpre"], bs["vpre"] + gi + 1)
                        sc.activation(sq[:, i:i + 1], var16[:, i:i + 1], AF.Ln,
                                      bias=LN_EPS)
                        sc.activation(rstd16[:, i:i + 1], sq[:, i:i + 1], AF.Exp,
                                      scale=-0.5,
                                      inc=(S["srstd"], 1, bs["srstd"] + gi + 1))
                    for w in range(32):
                        sc.wait_ge(S["spje"], bs["spje"] + b_ * 96 + 64 + w + 1)
                        oc, st4 = divmod(w, 4)
                        d_qo.start(sc, qT_hbm[st4 * 4:(st4 + 1) * 4, oc]
                                   .rearrange("c p n -> p c n"),
                                   qstg[:, (w % 2) * 512:(w % 2) * 512 + 512]
                                   .rearrange("p (c n) -> p c n", c=4),
                                   b_ * 32 + w)
                    for i in range(16):
                        gi = b_ * 16 + i
                        sc.wait_ge(S["vnm"], bs["vnm"] + gi + 1)
                        if gi >= 1:
                            sc.wait_ge(S["tpt"], bs["tpt"] + 2 * gi)
                        sc.activation(pn_v[:, :], ps_e[:, :], AF.Exp,
                                      bias=nmx[:, 0:1], accum_out=nmx[:, 1:2],
                                      inc=(S["spx"], 1, bs["spx"] + gi + 1))
                        if i >= 1:
                            sc.wait_ge(S["vfin"], bs["vfin"] + 2 * gi)
                            d_st.start(sc, xtile(x_dst, b_, i - 1),
                                       xr[:, ((gi - 1) % 2) * 1024:
                                          ((gi - 1) % 2) * 1024 + 1024], gi - 1)
                    gi = b_ * 16 + 15
                    sc.wait_ge(S["vfin"], bs["vfin"] + 2 * (gi + 1))
                    d_st.start(sc, xtile(x_dst, b_, 15),
                               xr[:, (gi % 2) * 1024:(gi % 2) * 1024 + 1024], gi)
                d_st.wait_all(sc)

            @blk.vector
            def _(v):
                v = Ser(env, "v", v)
                for b_ in range(nb):
                    for i in range(16):
                        gi = b_ * 16 + i
                        d_lx.wait(v, gi)
                        v.reduce_sum(sx[:, i:i + 1],
                                     xs[:, (gi % 2) * 1024:(gi % 2) * 1024 + 1024],
                                     axis=AX.X)
                        v.wait_ge(S["slnsq"], bs["slnsq"] + gi + 1)
                        v.tensor_scalar(mu16[:, i:i + 1], sx[:, i:i + 1],
                                        1.0 / D, None, op0=OP.mult)
                        v.tensor_tensor(sx[:, i:i + 1], mu16[:, i:i + 1],
                                        mu16[:, i:i + 1], op=OP.mult)
                        v.tensor_scalar(var16[:, i:i + 1], sxx[:, i:i + 1],
                                        1.0 / D, None, op0=OP.mult)
                        v.tensor_tensor(var16[:, i:i + 1], var16[:, i:i + 1],
                                        sx[:, i:i + 1], op=OP.subtract,
                                        inc=(S["vpre"], 1, bs["vpre"] + gi + 1))
                        v.wait_ge(S["srstd"], bs["srstd"] + gi + 1)
                        if gi >= 1:
                            v.wait_ge(S["txpd"], bs["txpd"] + 2 * gi)
                        v.tensor_scalar(xln[:, :],
                                        xs[:, (gi % 2) * 1024:(gi % 2) * 1024 + 1024],
                                        mu16[:, i:i + 1], rstd16[:, i:i + 1],
                                        op0=OP.subtract, op1=OP.mult,
                                        inc=(S["vln"], 1, bs["vln"] + gi + 1))
                        full = xlnT[:, :].rearrange("p (c w) -> p c w", c=8)
                        for hb in range(2):
                            v.wait_ge(S["txpd"], bs["txpd"] + 2 * gi + hb + 1)
                            v.tensor_copy(
                                full[:, hb * 4:(hb + 1) * 4, i * 128:(i + 1) * 128],
                                ps_av[hb][:, :].rearrange("p (c w) -> p c w", c=4),
                                inc=(S["vxpe"], 1, bs["vxpe"] + 2 * gi + hb + 1))
                    for u_ in range(96):
                        sl = b_ * 96 + u_
                        v.wait_ge(S["tpj"], bs["tpj"] + sl + 1)
                        if u_ < 32:
                            half, ti = divmod(u_, 16)
                            dst = u_bf[:, ti * 1024 + half * 512:
                                       ti * 1024 + half * 512 + 512]
                        elif u_ < 64:
                            w = u_ - 32
                            oc, st4 = divmod(w, 4)
                            dst = akT[:, oc * 2048 + st4 * 512:
                                      oc * 2048 + st4 * 512 + 512]
                        else:
                            w = u_ - 64
                            if w >= 2:
                                d_qo.wait(v, b_ * 32 + w - 2)
                            dst = qstg[:, (w % 2) * 512:(w % 2) * 512 + 512]
                        v.tensor_copy(dst, ps_av[sl % 2][:, :],
                                      inc=(S["spje"], 1, bs["spje"] + sl + 1))
                    for i in range(16):
                        gi = b_ * 16 + i
                        v.wait_ge(S["te"], bs["te"] + 4 * (gi + 1))
                        v.reduce_max(nmx[:, 0:1], ps_e[:, :], axis=AX.X,
                                     negate=True,
                                     inc=(S["vnm"], 1, bs["vnm"] + gi + 1))
                        v.wait_ge(S["spx"], bs["spx"] + gi + 1)
                        v.reciprocal(nmx[:, 2:3], nmx[:, 1:2])
                        for h2 in range(2):
                            v.wait_ge(S["tpt"], bs["tpt"] + 2 * gi + h2 + 1)
                            v.tensor_copy(pt_v[:, h2 * 1024:(h2 + 1) * 1024],
                                          ps_pt[h2][:, :],
                                          inc=(S["vpte"], 1,
                                               bs["vpte"] + 2 * gi + h2 + 1))
                        for half in range(2):
                            ju = 2 * gi + half
                            v.wait_ge(S["tav"], bs["tav"] + ju + 1)
                            d_lo.wait(v, gi)
                            v.scalar_tensor_tensor(
                                xr[:, (gi % 2) * 1024 + half * 512:
                                   (gi % 2) * 1024 + half * 512 + 512],
                                ps_av[ju % 2][:, :], nmx[:, 2:3],
                                xr[:, (gi % 2) * 1024 + half * 512:
                                   (gi % 2) * 1024 + half * 512 + 512],
                                op0=OP.mult, op1=OP.add,
                                inc=(S["vfin"], 1, bs["vfin"] + ju + 1))

            @blk.tensor
            def _(t):
                for b_ in range(nb):
                    for i in range(16):
                        gi = b_ * 16 + i
                        t.wait_ge(S["vln"], bs["vln"] + gi + 1)
                        if i == 0 and b_ >= 1:
                            t.wait_ge(S["vfin"], bs["vfin"] + 32 * b_)
                        for hb in range(2):
                            if 2 * gi + hb >= 2:
                                t.wait_ge(S["vxpe"], bs["vxpe"] + 2 * gi + hb - 1)
                            ins = None
                            for q in range(4):
                                dc = hb * 4 + q
                                ins = t.matmul(ps_av[hb][:, q * 128:(q + 1) * 128],
                                               xln[:, dc * 128:dc * 128 + 128],
                                               env.ident_f[:, :], is_transpose=True,
                                               start=True, stop=True,
                                               skip_group_check=True)
                            ins.then_inc(S["txpd"], 1)
                    t.wait_ge(S["vxpe"], bs["vxpe"] + 32 * (b_ + 1))
                    for half in range(2):
                        t.wait_ge(S["ldvb"], bs["ldvb"] + 16 * (2 * b_ + half + 1))
                        for ti in range(16):
                            sl = b_ * 96 + half * 16 + ti
                            if sl >= 2:
                                t.wait_ge(S["spje"], bs["spje"] + sl - 1)
                            ins = None
                            for dc in range(8):
                                ins = t.matmul(ps_av[sl % 2][:, :],
                                               xlnT[:, dc * 2048 + ti * 128:
                                                    dc * 2048 + ti * 128 + 128],
                                               wvb_v[:, dc * 512:(dc + 1) * 512],
                                               start=(dc == 0), stop=(dc == 7))
                            ins.then_inc(S["tpj"], 1)
                    for m in range(2):
                        for oc in range(8):
                            gi = b_ * 16 + m * 8 + oc
                            d_wt.wait(t, gi)
                            for st4 in range(4):
                                sl = b_ * 96 + 32 + m * 32 + oc * 4 + st4
                                if sl >= 2:
                                    t.wait_ge(S["spje"], bs["spje"] + sl - 1)
                                ins = None
                                for dc in range(8):
                                    ins = t.matmul(
                                        ps_av[sl % 2][:, :],
                                        wbuf[:, (gi % 2) * 1024 + dc * 128:
                                             (gi % 2) * 1024 + dc * 128 + 128],
                                        xlnT[:, dc * 2048 + st4 * 512:
                                             dc * 2048 + st4 * 512 + 512],
                                        start=(dc == 0), stop=(dc == 7))
                                ins.then_inc(S["tpj"], 1)
                    for i in range(16):
                        gi = b_ * 16 + i
                        d_ql.wait(t, gi)
                        if gi >= 1:
                            t.wait_ge(S["spx"], bs["spx"] + gi)
                        for js in range(4):
                            ins = None
                            for qc in range(8):
                                ins = t.matmul(ps_e[:, js * 512:(js + 1) * 512],
                                               qt_v[:, (gi % 2) * 1024 + qc * 128:
                                                    (gi % 2) * 1024 + qc * 128 + 128]
                                               ,
                                               akT[:, qc * 2048 + js * 512:
                                                   qc * 2048 + js * 512 + 512],
                                               start=(qc == 0), stop=(qc == 7))
                            ins.then_inc(S["te"], 1)
                        t.wait_ge(S["spx"], bs["spx"] + gi + 1)
                        if gi >= 1:
                            t.wait_ge(S["vpte"], bs["vpte"] + 2 * gi)
                        for h2 in range(2):
                            ins = None
                            for jc in range(8):
                                ins = t.matmul(
                                    ps_pt[h2][:, jc * 128:(jc + 1) * 128],
                                    pn_v[:, h2 * 1024 + jc * 128:
                                         h2 * 1024 + jc * 128 + 128],
                                    env.ident_bf[:, :], is_transpose=True,
                                    start=True, stop=True, skip_group_check=True)
                            ins.then_inc(S["tpt"], 1)
                        for half in range(2):
                            ju = 2 * gi + half
                            t.wait_ge(S["vpte"], bs["vpte"] + 2 * gi + 2)
                            if ju >= 2:
                                t.wait_ge(S["vfin"], bs["vfin"] + ju - 1)
                            ins = None
                            for jc in range(16):
                                ins = t.matmul(ps_av[ju % 2][:, :],
                                               pt_v[:, jc * 128:(jc + 1) * 128],
                                               u_bf[:, jc * 1024 + half * 512:
                                                    jc * 1024 + half * 512 + 512],
                                               start=(jc == 0), stop=(jc == 15))
                            ins.then_inc(S["tav"], 1)

        d_lx.done()
        d_lo.done()
        d_ql.done()
        d_st.done()
        d_wt.done()
        d_qo.done()
        env.base["a_vpre"] += 16 * nb
        env.base["a_srstd"] += 16 * nb
        env.base["a_vln"] += 16 * nb
        env.base["a_slnsq"] += 16 * nb
        env.base["a_txpd"] += 32 * nb
        env.base["a_vxpe"] += 32 * nb
        env.base["a_tpj"] += 96 * nb
        env.base["a_spje"] += 96 * nb
        env.base["a_te"] += 64 * nb
        env.base["a_vnm"] += 16 * nb
        env.base["a_spx"] += 16 * nb
        env.base["a_tpt"] += 32 * nb
        env.base["a_vpte"] += 32 * nb
        env.base["a_tav"] += 32 * nb
        env.base["a_vfin"] += 32 * nb
        env.base["a_ldvb"] += 32 * nb


def build_l1attn_program(nb):
    nc = bass.Bass("TRN2", target_bir_lowering=False)
    x_in = nc.dram_tensor("x_in", [nb * T, D], F32, kind="ExternalInput").ap()
    wk = nc.dram_tensor("L1_attn_Wk", [D, D], F32, kind="ExternalInput").ap()
    wq = nc.dram_tensor("L1_attn_Wq", [D, D], F32, kind="ExternalInput").ap()
    wv = nc.dram_tensor("L1_attn_Wv", [D, D], F32, kind="ExternalInput").ap()
    wa = nc.dram_tensor("L1_attn_Wa", [D, D], F32, kind="ExternalInput").ap()
    wb = nc.dram_tensor("L1_attn_Wb", [D, D], F32, kind="ExternalInput").ap()
    x_out = nc.dram_tensor("x_out", [nb * T, D], F32, kind="ExternalOutput").ap()
    with ExitStack() as stack:
        env = Env(nc, stack)
        emit_consts(env)
        emit_l1attn_prep(env, wk, wq, wv, wa, wb)
        emit_l1attn(env, x_in, x_out, nb, "at1")
    return nc






H0 = 16
DH = 64
PD = 4
SCALE0 = DH ** (-0.5)
_PAIRS = []
_SGN = [1, -1, 1, 1, -1, 1]
for _i in range(6):
    for _j in range(_i, 6):
        _PAIRS.append((_i, _j, (1.0 if _i == _j else 2.0) * _SGN[_i] * _SGN[_j]))
NPAIR = len(_PAIRS)
_PI = [0, 0, 0, 1, 1, 2]
_PJ = [1, 2, 3, 2, 3, 3]


def emit_l0attn_prep(env, decay_ap, ms_ap, kvg_ap, qkvb_ap, outb_ap):
    nc = env.nc
    env.MlowT = env.sbuf("o_MlowT", [128, 16 * 128], BF16)
    env.dPow = env.sbuf("o_dPow", [1, 16 * 128], BF16)
    env.dRev = env.sbuf("o_dRev", [128, 16], BF16)
    env.d128b = env.sbuf("o_d128b", [1, 16], BF16)
    env.kvg = env.sbuf("o_kvg", [128, 1], F32)
    env.trineg = env.sbuf("o_trineg", [128, 128], F32)
    env.qkvb_row = env.sbuf("o_qkvbr", [1, 3072], BF16)
    env.outb_row = env.sbuf("o_outbr", [1, 1024], BF16)
    env.qb_cols = emit_vec_to_cols(env, qkvb_ap[0:1024], 1024, "o_qb")["f32"]
    ls = ExitStack()
    rows = env.sbuf("o_prows", [1, 64], F32, stack=ls)
    lndrep = env.sbuf("o_lndrep", [128, 16], F32, stack=ls)
    msrep = env.sbuf("o_msrep", [128, 16], F32, stack=ls)
    iod = env.sbuf("o_iod", [128, 128], F32, stack=ls)
    iorow = env.sbuf("o_iorow", [1, 128], F32, stack=ls)
    iorev = env.sbuf("o_iorev", [128, 1], F32, stack=ls)
    mtmp = env.sbuf("o_mtmp", [128, 16 * 128], F32, stack=ls)
    rtmp = env.sbuf("o_rtmp", [1, 16 * 128], F32, stack=ls)
    ctmp = env.sbuf("o_ctmp", [128, 16], F32, stack=ls)
    ps = env.psum("o_pps", [128, 64], F32, stack=ls)
    d = env.sem("op_d")
    gp = env.sem("op_g")
    ts = env.sem("op_t")
    bd, bg, bt = env.base["op_d"], env.base["op_g"], env.base["op_t"]

    with nc.Block("l0ap_s1") as blk:
        @blk.sync
        def _(s):
            s.dma_start(rows[0:1, 0:16],
                        decay_ap.rearrange("(o n) -> o n", o=1)).then_inc(d, 16)
            s.dma_start(rows[0:1, 16:32],
                        ms_ap.rearrange("(o n) -> o n", o=1)).then_inc(d, 16)
            s.dma_start(rows[0:1, 32:33],
                        kvg_ap.rearrange("() -> 1 1") if False else
                        bass.AP(kvg_ap.tensor, 0, [[1, 1], [1, 1]])).then_inc(d, 16)
            s.wait_ge(d, bd + 48)
        @blk.gpsimd
        def _(g):
            g = Ser(env, "g", g)
            g.dma_start(env.qkvb_row[:, :],
                        qkvb_ap.rearrange("(o n) -> o n", o=1)).then_inc(gp, 16)
            g.dma_start(env.outb_row[:, :],
                        outb_ap.rearrange("(o n) -> o n", o=1)).then_inc(gp, 16)
            g.iota(iod[:, :], [[1, 128]], base=0, channel_multiplier=-1,
                   allow_small_or_imprecise_dtypes=True)
            g.iota(iorow[:, :], [[1, 128]], base=0, channel_multiplier=0,
                   allow_small_or_imprecise_dtypes=True)
            g.iota(iorev[:, :], [[0, 1]], base=128, channel_multiplier=-1,
                   allow_small_or_imprecise_dtypes=True)
            g.memset(env.trineg[:, :], 0.0)
            g.affine_select(env.trineg[:, :], env.trineg[:, :], [[-1, 128]],
                            compare_op=OP.is_ge, fill=-1e30,
                            base=0, channel_multiplier=1)
            g.wait_ge(gp, bg + 32)
    env.base["op_d"] += 48
    env.base["op_g"] += 32
    with nc.Block("l0ap_s2") as blk:
        @blk.scalar
        def _(sc):
            sc = Ser(env, "s", sc)
            sc.activation(rows[0:1, 0:16], rows[0:1, 0:16], AF.Sigmoid)
        @blk.vector
        def _(v):
            v = Ser(env, "v", v)
            pass
    with nc.Block("l0ap_s3") as blk:
        @blk.vector
        def _(v):
            v = Ser(env, "v", v)
            v.tensor_scalar(rows[0:1, 0:16], rows[0:1, 0:16], 1e-30, None,
                            op0=OP.max)
    with nc.Block("l0ap_s4") as blk:
        @blk.scalar
        def _(sc):
            sc = Ser(env, "s", sc)
            sc.activation(rows[0:1, 0:16], rows[0:1, 0:16], AF.Ln)
    with nc.Block("l0ap_s5") as blk:
        @blk.tensor
        def _(t):
            t.matmul(ps[:, 0:16], env.onesrow_f[:, :], rows[0:1, 0:16],
                     start=True, stop=True)
            t.matmul(ps[:, 16:32], env.onesrow_f[:, :], rows[0:1, 16:32],
                     start=True, stop=True)
            t.matmul(ps[:, 32:33], env.onesrow_f[:, :], rows[0:1, 32:33],
                     start=True, stop=True).then_inc(ts, 1)
        @blk.vector
        def _(v):
            v = Ser(env, "v", v)
            v.wait_ge(ts, bt + 1)
            v.tensor_copy(lndrep[:, :], ps[:, 0:16])
            v.tensor_copy(msrep[:, :], ps[:, 16:32])
            v.tensor_copy(env.kvg[:, :], ps[:, 32:33])
    env.base["op_t"] += 1
    with nc.Block("l0ap_s6") as blk:
        @blk.vector
        def _(v):
            v = Ser(env, "v", v)
            for h in range(H0):
                v.tensor_scalar(mtmp[:, h * 128:(h + 1) * 128], iod[:, :],
                                lndrep[:, h:h + 1], None, op0=OP.mult)
                v.tensor_scalar(rtmp[0:1, h * 128:(h + 1) * 128], iorow[0:1, :],
                                rows[0:1, h:h + 1], None, op0=OP.mult)
                v.tensor_scalar(ctmp[:, h:h + 1], iorev[:, :],
                                lndrep[:, h:h + 1], None, op0=OP.mult)
            v.tensor_scalar(rows[0:1, 48:64], rows[0:1, 0:16], 128.0, None,
                            op0=OP.mult)
    with nc.Block("l0ap_s7") as blk:
        @blk.scalar
        def _(sc):
            sc = Ser(env, "s", sc)
            sc.activation(mtmp[:, :], mtmp[:, :], AF.Exp)
            sc.activation(rtmp[0:1, :], rtmp[0:1, :], AF.Exp)
            sc.activation(ctmp[:, :], ctmp[:, :], AF.Exp)
            sc.activation(rows[0:1, 48:64], rows[0:1, 48:64], AF.Exp)
    with nc.Block("l0ap_s8") as blk:
        @blk.vector
        def _(v):
            v = Ser(env, "v", v)
            for h in range(H0):
                v.tensor_scalar(mtmp[:, h * 128:(h + 1) * 128],
                                mtmp[:, h * 128:(h + 1) * 128],
                                msrep[:, h:h + 1], None, op0=OP.mult)
            v.tensor_tensor(ctmp[:, :], ctmp[:, :], msrep[:, :], op=OP.mult)
    with nc.Block("l0ap_s9") as blk:
        @blk.gpsimd
        def _(g):
            g = Ser(env, "g", g)
            g.affine_select(mtmp[:, :], mtmp[:, :], [[0, 16], [1, 128]],
                            compare_op=OP.is_gt, fill=0.0,
                            base=0, channel_multiplier=-1)
    with nc.Block("l0ap_s10") as blk:
        @blk.vector
        def _(v):
            v = Ser(env, "v", v)
            v.tensor_copy(env.MlowT[:, :], mtmp[:, :])
            v.tensor_copy(env.dPow[0:1, :], rtmp[0:1, :])
            v.tensor_copy(env.dRev[:, :], ctmp[:, :])
            v.tensor_copy(env.d128b[0:1, :], rows[0:1, 48:64])
    ls.close()


def emit_l0attn(env, x_src, x_dst, nb, tag, qkvw_ap, kvgeow_ap, outw_ap,
                w1w_ap, w2w_ap, r1_ap, r2_ap):
    nc = env.nc
    PADW = 2176
    qT_hbm = nc.dram_tensor(tag + "_qT", [16, 8, 128, 128], BF16,
                            kind="Internal").ap()
    with ExitStack() as ls:
        xlnT = env.sbuf(tag + "_xlnT", [128, 8 * PADW], BF16, stack=ls)
        scrA = env.sbuf(tag + "_scrA", [128, 20736], BF16, stack=ls)
        vE = env.sbuf(tag + "_vE", [128, 16384], BF16, stack=ls)
        kT = env.sbuf(tag + "_kT", [128, 16384], BF16, stack=ls)
        wbig = env.sbuf(tag + "_wbig", [128, 16384], BF16, stack=ls)
        qtile = env.sbuf(tag + "_qtile", [128, 1024], BF16, stack=ls)
        qstg = env.sbuf(tag + "_qstg", [128, 2 * 512], BF16, stack=ls)
        sqq = env.sbuf(tag + "_sqq", [128, 512], BF16, stack=ls)
        xs = env.sbuf(tag + "_xs", [128, 1024], F32, stack=ls)
        xr = env.sbuf(tag + "_xr", [128, 1024], F32, stack=ls)
        wwr = wbig[:, 12288:14336]
        gates = env.sbuf(tag + "_gates", [128, 256], F32, stack=ls)
        mem_all = env.sbuf(tag + "_mem", [128, 256], F32, stack=ls)
        mrow = env.sbuf(tag + "_mrow", [1, 64], F32, stack=ls)
        mrep = env.sbuf(tag + "_mrep", [128, 16], F32, stack=ls)
        st5 = env.sbuf(tag + "_st5", [128, 5 * 16], F32, stack=ls)
        ellp = env.sbuf(tag + "_ellp", [128, 8], F32, stack=ls)
        ell = env.sbuf(tag + "_ell", [128, 2], F32, stack=ls)
        C0b = scrA[0:1, 672:1008]
        tmpA = scrA[:, 17920:18432].bitcast(F32)
        tmpB = scrA[:, 18432:18944].bitcast(F32)
        nsq = scrA[:, 18944:19456].bitcast(F32)
        invn = scrA[:, 19456:19968].bitcast(F32)
        prod = scrA[:, 0:672].bitcast(F32)
        w4 = lambda: scrA[:, 0:4096]
        Lw = lambda m: scrA[:, 4096 + m * 256:4096 + (m + 1) * 256]
        Lr = lambda m: scrA[:, 5632 + m * 256:5632 + (m + 1) * 256]
        WpT = scrA[:, 7168:12544]
        RpT = scrA[:, 12544:17920]
        tmpf = scrA[:, 17920:18432].bitcast(F32)
        tmp2f = scrA[:, 18432:18944].bitcast(F32)
        nsqf = scrA[:, 18944:19456].bitcast(F32)
        Cbf = scrA[:, 19968:20640].bitcast(F32)
        k_nat = scrA[:, 0:16384]
        kgeo_t = scrA[:, 16384:17408]
        vgeo_t = scrA[:, 17408:18432]
        sqk = scrA[:, 17920:19968]
        P_v = [xlnT[:, 0:2048], xlnT[:, 2048:4096]]
        PT_v = [xlnT[:, 4096:6144], xlnT[:, 6144:8192]]
        ao_v = xlnT[:, 8192:9216]
        aoT_v = xlnT[:, 9216:10240]
        psA = env.psum(tag + "_psA", [128, 512], F32, stack=ls)
        psB = env.psum(tag + "_psB", [128, 512], F32, stack=ls)
        psC = env.psum(tag + "_psC", [128, 512], F32, stack=ls)
        psD = env.psum(tag + "_psD", [128, 512], F32, stack=ls)
        psE = env.psum(tag + "_psE", [128, 1024], BF16, stack=ls)
        psF = env.psum(tag + "_psF", [128, 1024], BF16, stack=ls)
        psG = env.psum(tag + "_psG", [128, 1024], BF16, stack=ls)
        psAB = [psA, psB]
        psEF = [psE, psF]

        sems = ["ldx", "vln", "slnsq", "vpre", "srstd", "txpd", "vxpe",
                "t2", "v2", "t3", "v3", "s3g", "v3g",
                "t4a", "v4a", "s4a", "t4n", "v4n",
                "t4k", "v4k", "t4c", "v4c", "t4x", "v4x", "t4m", "v4m", "s4m",
                "t5s", "v5m", "s5e", "v5l", "t5p", "v5p", "t5a", "s5a",
                "t5x", "v5x", "t5o", "v5o", "gw"]
        S = {k: env.sem("oa_" + k) for k in sems}
        B = {k: env.base["oa_" + k] for k in sems}
        d_lx = Dma2(env, "oa_lx")
        d_lo = Dma2(env, "oa_lo")
        d_ql = Dma2(env, "oa_ql")
        d_st = Dma2(env, "oa_st")
        d_qo = Dma2(env, "oa_qo")

        def xtile(ap, b_, i):
            return ap[b_ * T + i * 128:b_ * T + i * 128 + 128, :]

        def xlc(dc, off, width):
            return xlnT[:, dc * PADW + off:dc * PADW + off + width]

        nsf = lambda i: i // 4 + 1
        pkf = lambda i: (i + 8) // 8
        CNS = [0]
        CPK = [0]
        for _i in range(16):
            CNS.append(CNS[-1] + nsf(_i))
            CPK.append(CPK[-1] + pkf(_i))
        TNS = CNS[16]
        TPK = CPK[16]

        for b_ in range(nb):
            with nc.Block(f"{tag}_b{b_}_ln") as blk:
                @blk.sync
                def _(s, b_=b_):
                    for i in range(16):
                        gi = b_ * 16 + i
                        if gi >= 1:
                            s.wait_ge(S["vln"], B["vln"] + gi)
                        d_lx.start(s, xs[:, :], xtile(x_src, b_, i), gi)
                @blk.gpsimd
                def _(g, b_=b_):
                    g = Ser(env, "g", g)
                    if b_ == 0:
                        g.memset(xlnT[:, :].rearrange("p (c w) -> p c w", c=8)
                                 [:, :, 0:1], 0.0)
                    for m, w_ap in enumerate([w1w_ap, w2w_ap, r1_ap, r2_ap]):
                        g.dma_start(wwr[:, m * 512:(m + 1) * 512]
                                    .rearrange("p (c n) -> p c n", c=8),
                                    w_ap.rearrange("(c p) n -> p c n", p=128)
                                    ).then_inc(S["gw"], 16)
                @blk.scalar
                def _(sc, b_=b_):
                    sc = Ser(env, "s", sc)
                    for i in range(16):
                        gi = b_ * 16 + i
                        d_lx.wait(sc, gi)
                        sc.activation(sqk[:, 0:1024], xs[:, :], AF.Square,
                                      accum_out=st5[:, 16 + i:17 + i],
                                      inc=(S["slnsq"], 1, B["slnsq"] + gi + 1))
                        sc.wait_ge(S["vpre"], B["vpre"] + gi + 1)
                        sc.activation(st5[:, 64 + i:65 + i], st5[:, 48 + i:49 + i],
                                      AF.Ln, bias=LN_EPS)
                        sc.activation(st5[:, 64 + i:65 + i], st5[:, 64 + i:65 + i],
                                      AF.Exp, scale=-0.5,
                                      inc=(S["srstd"], 1, B["srstd"] + gi + 1))
                @blk.vector
                def _(v, b_=b_):
                    v = Ser(env, "v", v)
                    for i in range(16):
                        gi = b_ * 16 + i
                        d_lx.wait(v, gi)
                        v.reduce_sum(st5[:, i:i + 1], xs[:, :], axis=AX.X)
                        v.wait_ge(S["slnsq"], B["slnsq"] + gi + 1)
                        v.tensor_scalar(st5[:, 32 + i:33 + i], st5[:, i:i + 1],
                                        1.0 / D, None, op0=OP.mult)
                        v.tensor_tensor(st5[:, i:i + 1], st5[:, 32 + i:33 + i],
                                        st5[:, 32 + i:33 + i], op=OP.mult)
                        v.scalar_tensor_tensor(st5[:, 48 + i:49 + i],
                                               st5[:, 16 + i:17 + i], 1.0 / D,
                                               st5[:, i:i + 1],
                                               op0=OP.mult, op1=OP.subtract,
                                               inc=(S["vpre"], 1, B["vpre"] + gi + 1))
                        v.wait_ge(S["srstd"], B["srstd"] + gi + 1)
                        if gi >= 1:
                            v.wait_ge(S["txpd"], B["txpd"] + gi)
                        v.tensor_scalar(scrA[:, 0:1024], xs[:, :],
                                        st5[:, 32 + i:33 + i],
                                        st5[:, 64 + i:65 + i],
                                        op0=OP.subtract, op1=OP.mult,
                                        inc=(S["vln"], 1, B["vln"] + gi + 1))
                        v.wait_ge(S["txpd"], B["txpd"] + gi + 1)
                        full = xlnT[:, :].rearrange("p (c w) -> p c w", c=8)
                        v.tensor_copy(full[:, :, 1 + i * 128:1 + (i + 1) * 128],
                                      psEF[gi % 2][:, :].rearrange(
                                          "p (c w) -> p c w", c=8),
                                      inc=(S["vxpe"], 1, B["vxpe"] + gi + 1))
                @blk.tensor
                def _(t, b_=b_):
                    for i in range(16):
                        gi = b_ * 16 + i
                        t.wait_ge(S["vln"], B["vln"] + gi + 1)
                        if gi >= 2:
                            t.wait_ge(S["vxpe"], B["vxpe"] + gi - 1)
                        ins = None
                        for dc in range(8):
                            ins = t.matmul(psEF[gi % 2][:, dc * 128:(dc + 1) * 128],
                                           scrA[:, dc * 128:dc * 128 + 128],
                                           env.ident_bf[:, :], is_transpose=True,
                                           start=True, stop=True,
                                           skip_group_check=True)
                        ins.then_inc(S["txpd"], 1)
            def w4ap(m, p):
                return bass.AP(scrA, m * 64 + p, [[20736, 128], [256, 16], [4, 16]])

            def lap3(base, m):
                return bass.AP(scrA, base + m * 256,
                               [[20736, 128], [16, 16], [1, 16]])

            def prap(base, p):
                return bass.AP(scrA, base + p,
                               [[20736, 128], [336, 16], [21, 16]])

            t3A = tmpA.rearrange("p (t q) -> p t q", q=16)
            t3B = tmpB.rearrange("p (t q) -> p t q", q=16)
            with nc.Block(f"{tag}_b{b_}_wl") as blk:
                @blk.tensor
                def _(t, b_=b_):
                    t.wait_ge(S["gw"], B["gw"] + 128 * b_ + 64)
                    for i in range(16):
                        if i >= 2:
                            t.wait_ge(S["v2"], B["v2"] + b_ * 18 + i - 1)
                        ins = None
                        for m in range(4):
                            off = 0 if m == 0 else 1
                            for dc in range(8):
                                ins = t.matmul(
                                    psAB[i % 2][:, m * 64:(m + 1) * 64],
                                    xlc(dc, off + i * 128, 128),
                                    wwr[:, m * 512 + dc * 64:m * 512 + dc * 64 + 64],
                                    start=(dc == 0), stop=(dc == 7),
                                    skip_group_check=True)
                        ins.then_inc(S["t2"], 1)
                @blk.vector
                def _(v, b_=b_):
                    v = Ser(env, "v", v)
                    for i in range(16):
                        v.wait_ge(S["t2"], B["t2"] + b_ * 16 + i + 1)
                        v.tensor_copy(scrA[:, i * 256:(i + 1) * 256],
                                      psAB[i % 2][:, 0:256],
                                      inc=(S["v2"], 1, B["v2"] + b_ * 18 + i + 1))
                    for side in range(2):
                        first, second = (0, 1) if side == 0 else (2, 3)
                        lbase = 4096 if side == 0 else 5632
                        for m in range(6):
                            pi, pj = _PI[m], _PJ[m]
                            v.tensor_tensor(t3A, w4ap(first, pi), w4ap(second, pj),
                                            op=OP.mult)
                            v.tensor_tensor(t3B, w4ap(first, pj), w4ap(second, pi),
                                            op=OP.mult)
                            v.tensor_tensor(lap3(lbase, m), t3A, t3B,
                                            op=OP.subtract)
                        L2 = lambda m: scrA[:, lbase + m * 256:lbase + (m + 1) * 256]
                        v.tensor_tensor(nsq, L2(0), L2(0), op=OP.mult)
                        for m in range(1, 6):
                            v.tensor_tensor(tmpA, L2(m), L2(m), op=OP.mult)
                            v.tensor_tensor(nsq, nsq, tmpA,
                                            op=OP.add)
                        v.tensor_scalar(nsq, nsq, 1e-18, None,
                                        op0=OP.max,
                                        inc=(S["v2"], 1,
                                             B["v2"] + b_ * 18 + 17 + side))
                        v.wait_ge(S["s3g"], B["s3g"] + b_ * 3 + side + 1)
                        for m in range(6):
                            v.tensor_tensor(L2(m), L2(m), invn, op=OP.mult)
                    for idx, (pi_, pj_, c_) in enumerate(_PAIRS):
                        v.tensor_tensor(prap(7168, idx), lap3(4096, 5 - pi_),
                                        lap3(4096, 5 - pj_), op=OP.mult)
                        v.scalar_tensor_tensor(prap(12544, idx), lap3(5632, pi_),
                                               c_, lap3(5632, pj_),
                                               op0=OP.mult, op1=OP.mult)
                @blk.scalar
                def _(sc, b_=b_):
                    sc = Ser(env, "s", sc)
                    for side in range(2):
                        sc.wait_ge(S["v2"], B["v2"] + b_ * 18 + 17 + side)
                        sc.activation(invn, nsq, AF.Ln)
                        sc.activation(invn, invn, AF.Exp, scale=-0.5,
                                      inc=(S["s3g"], 1, B["s3g"] + b_ * 3 + side + 1))

            with nc.Block(f"{tag}_b{b_}_scan") as blk:
                @blk.vector
                def _(v, b_=b_):
                    v = Ser(env, "v", v)
                    v.memset(C0b[0:1, :], 0.0, inc=(S["v3"], 1, B["v3"] + b_ * 17 + 1))
                    for i in range(16):
                        v.wait_ge(S["t3"], B["t3"] + b_ * 16 + i + 1)
                        v.tensor_copy(Cbf, psC[:, 0:336])
                        v.tensor_copy(C0b[0:1, :], psD[0:1, 0:336])
                        v.tensor_tensor(prod, scrA[:, 12544 + i * 336:
                                                         12544 + (i + 1) * 336],
                                        Cbf, op=OP.mult)
                        for h in range(H0):
                            v.reduce_sum(mem_all[:, i * 16 + h:i * 16 + h + 1],
                                         prod[:, h * 21:(h + 1) * 21], axis=AX.X,
                                         inc=((S["v3"], 1, B["v3"] + b_ * 17 + i + 2)
                                              if h == H0 - 1 else None))
                    v.wait_ge(S["s3g"], B["s3g"] + b_ * 3 + 3)
                    v.tensor_scalar(gates[:, :], mem_all[:, :], env.kvg[:, 0:1],
                                    None, op0=OP.mult)
                @blk.tensor
                def _(t, b_=b_):
                    for i in range(16):
                        t.wait_ge(S["v3"], B["v3"] + b_ * 17 + i + 1)
                        for h in range(H0):
                            t.matmul(psC[:, h * 21:(h + 1) * 21],
                                     env.MlowT[:, h * 128:(h + 1) * 128],
                                     scrA[:, 7168 + i * 336 + h * 21:
                                          7168 + i * 336 + (h + 1) * 21],
                                     start=(h == 0), stop=False,
                                     skip_group_check=True)
                            t.matmul(psC[:, h * 21:(h + 1) * 21],
                                     env.dPow[0:1, h * 128:(h + 1) * 128],
                                     C0b[0:1, h * 21:(h + 1) * 21],
                                     start=False, stop=False,
                                     skip_group_check=True)
                            t.matmul(psD[0:1, h * 21:(h + 1) * 21],
                                     env.dRev[:, h:h + 1],
                                     scrA[:, 7168 + i * 336 + h * 21:
                                          7168 + i * 336 + (h + 1) * 21],
                                     start=(h == 0), stop=False,
                                     skip_group_check=True)
                            ins = t.matmul(psD[0:1, h * 21:(h + 1) * 21],
                                           env.d128b[0:1, h:h + 1],
                                           C0b[0:1, h * 21:(h + 1) * 21],
                                           start=False, stop=False,
                                           skip_group_check=True)
                        ins.then_inc(S["t3"], 1)
                @blk.scalar
                def _(sc, b_=b_):
                    sc = Ser(env, "s", sc)
                    sc.wait_ge(S["v3"], B["v3"] + b_ * 17 + 17)
                    sc.activation(mem_all[:, :], mem_all[:, :], AF.Sigmoid,
                                  inc=(S["s3g"], 1, B["s3g"] + b_ * 3 + 3))
            with nc.Block(f"{tag}_b{b_}_q") as blk:
                @blk.gpsimd
                def _(g, b_=b_):
                    g = Ser(env, "g", g)
                    g.dma_start(wbig[:, 0:8192].rearrange("p (c n) -> p c n", c=8),
                                qkvw_ap[:, 0:1024].rearrange(
                                    "(c p) n -> p c n", p=128)).then_inc(S["gw"], 16)
                @blk.tensor
                def _(t, b_=b_):
                    t.wait_ge(S["gw"], B["gw"] + 128 * b_ + 80)
                    for u_ in range(32):
                        oc, st4 = divmod(u_, 4)
                        sl = b_ * 32 + u_
                        if u_ >= 2:
                            t.wait_ge(S["v4a"], B["v4a"] + sl - 1)
                        ins = None
                        for dc in range(8):
                            ins = t.matmul(psAB[u_ % 2][:, :],
                                           wbig[:, dc * 1024 + oc * 128:
                                                dc * 1024 + oc * 128 + 128],
                                           xlc(dc, 1 + st4 * 512, 512),
                                           start=(dc == 0), stop=(dc == 7))
                        ins.then_inc(S["t4a"], 1)
                        if u_ >= 1:
                            w = u_ - 1
                            t.wait_ge(S["s4a"], B["s4a"] + b_ * 32 + w + 1)
                            if w >= 1:
                                t.wait_ge(S["v4n"], B["v4n"] + b_ * 32 + w)
                            t.matmul(psC[0:1, 0:512], env.onescol_bf[0:64, 0:1],
                                     sqq[0:64, :], start=True, stop=True,
                                     skip_group_check=True)
                            t.matmul(psD[0:1, 0:512], env.onescol_bf[64:128, 0:1],
                                     sqq[64:128, :], tile_position=(64, 0),
                                     start=True, stop=True,
                                     skip_group_check=True).then_inc(S["t4n"], 1)
                    w = 31
                    t.wait_ge(S["s4a"], B["s4a"] + b_ * 32 + 32)
                    t.wait_ge(S["v4n"], B["v4n"] + b_ * 32 + 31)
                    t.matmul(psC[0:1, 0:512], env.onescol_bf[0:64, 0:1],
                             sqq[0:64, :], start=True, stop=True,
                             skip_group_check=True)
                    t.matmul(psD[0:1, 0:512], env.onescol_bf[64:128, 0:1],
                             sqq[64:128, :], tile_position=(64, 0),
                             start=True, stop=True,
                             skip_group_check=True).then_inc(S["t4n"], 1)
                @blk.vector
                def _(v, b_=b_):
                    v = Ser(env, "v", v)
                    v.memset(mrow[0:1, 0:48], -1e30)
                    for u_ in range(32):
                        oc, st4 = divmod(u_, 4)
                        sl = b_ * 32 + u_
                        v.wait_ge(S["t4a"], B["t4a"] + sl + 1)
                        if u_ >= 2:
                            d_qo.wait(v, b_ * 32 + u_ - 2)
                        v.tensor_scalar(qstg[:, (u_ % 2) * 512:(u_ % 2) * 512 + 512],
                                        psAB[u_ % 2][:, :], env.qb_cols[:, oc:oc + 1],
                                        None, op0=OP.add,
                                        inc=(S["v4a"], 1, B["v4a"] + sl + 1))
                        if u_ >= 1:
                            v.wait_ge(S["t4n"], B["t4n"] + b_ * 32 + u_)
                            oc2 = (u_ - 1) // 4
                            v.reduce_max(ell[0:1, 0:1], psC[0:1, 0:512],
                                         axis=AX.X)
                            v.tensor_tensor(mrow[0:1, 2 * oc2:2 * oc2 + 1],
                                            mrow[0:1, 2 * oc2:2 * oc2 + 1],
                                            ell[0:1, 0:1], op=OP.max)
                            v.reduce_max(ell[0:1, 0:1], psD[0:1, 0:512], axis=AX.X)
                            v.tensor_tensor(mrow[0:1, 2 * oc2 + 1:2 * oc2 + 2],
                                            mrow[0:1, 2 * oc2 + 1:2 * oc2 + 2],
                                            ell[0:1, 0:1], op=OP.max,
                                            inc=(S["v4n"], 1, B["v4n"] + b_ * 32 + u_))
                    v.wait_ge(S["t4n"], B["t4n"] + b_ * 32 + 32)
                    oc2 = 7
                    v.reduce_max(ell[0:1, 0:1], psC[0:1, 0:512], axis=AX.X)
                    v.tensor_tensor(mrow[0:1, 2 * oc2:2 * oc2 + 1],
                                    mrow[0:1, 2 * oc2:2 * oc2 + 1],
                                    ell[0:1, 0:1], op=OP.max)
                    v.reduce_max(ell[0:1, 0:1], psD[0:1, 0:512], axis=AX.X)
                    v.tensor_tensor(mrow[0:1, 2 * oc2 + 1:2 * oc2 + 2],
                                    mrow[0:1, 2 * oc2 + 1:2 * oc2 + 2],
                                    ell[0:1, 0:1], op=OP.max,
                                    inc=(S["v4n"], 1, B["v4n"] + b_ * 32 + 32))
                @blk.scalar
                def _(sc, b_=b_):
                    sc = Ser(env, "s", sc)
                    for u_ in range(32):
                        oc, st4 = divmod(u_, 4)
                        sl = b_ * 32 + u_
                        sc.wait_ge(S["v4a"], B["v4a"] + sl + 1)
                        if u_ >= 1:
                            sc.wait_ge(S["t4n"], B["t4n"] + b_ * 32 + u_)
                        sc.activation(sqq[:, :],
                                      qstg[:, (u_ % 2) * 512:(u_ % 2) * 512 + 512],
                                      AF.Square, inc=(S["s4a"], 1, B["s4a"] + sl + 1))
                        d_qo.start(sc, qT_hbm[st4 * 4:(st4 + 1) * 4, oc]
                                   .rearrange("c p n -> p c n"),
                                   qstg[:, (u_ % 2) * 512:(u_ % 2) * 512 + 512]
                                   .rearrange("p (c n) -> p c n", c=4),
                                   b_ * 32 + u_)
            with nc.Block(f"{tag}_b{b_}_kv") as blk:
                @blk.gpsimd
                def _(g, b_=b_):
                    g = Ser(env, "g", g)
                    g.dma_start(wbig[:, :].rearrange("p (c n) -> p c n", c=8),
                                qkvw_ap[:, 1024:3072].rearrange(
                                    "(c p) n -> p c n", p=128)).then_inc(S["gw"], 16)
                @blk.tensor
                def _(t, b_=b_):
                    t.wait_ge(S["gw"], B["gw"] + 128 * b_ + 96)
                    for i in range(16):
                        if i >= 1:
                            t.wait_ge(S["v4k"], B["v4k"] + b_ * 16 + i)
                        for r, psq in enumerate([psA, psB, psC, psD]):
                            half = r % 2
                            isv = r // 2
                            ins = None
                            for dc in range(8):
                                ins = t.matmul(
                                    psq[:, :],
                                    xlc(dc, 1 + i * 128, 128),
                                    wbig[:, dc * 2048 + isv * 1024 + half * 512:
                                         dc * 2048 + isv * 1024 + half * 512 + 512],
                                    start=(dc == 0), stop=False)
                            t.matmul(psq[:, :], env.onesrow_bf[:, :],
                                     env.qkvb_row[0:1,
                                                  1024 + isv * 1024 + half * 512:
                                                  1024 + isv * 1024 + half * 512 + 512],
                                     start=False, stop=True).then_inc(S["t4k"], 1)
                @blk.vector
                def _(v, b_=b_):
                    v = Ser(env, "v", v)
                    for i in range(16):
                        v.wait_ge(S["t4k"], B["t4k"] + (b_ * 16 + i + 1) * 4)
                        v.tensor_copy(scrA[:, i * 1024:i * 1024 + 512], psA[:, :])
                        v.tensor_copy(scrA[:, i * 1024 + 512:(i + 1) * 1024],
                                      psB[:, :])
                        v.tensor_copy(vE[:, i * 1024:i * 1024 + 512], psC[:, :])
                        v.tensor_copy(vE[:, i * 1024 + 512:(i + 1) * 1024],
                                      psD[:, :],
                                      inc=(S["v4k"], 1, B["v4k"] + b_ * 16 + i + 1))

            with nc.Block(f"{tag}_b{b_}_geo") as blk:
                @blk.gpsimd
                def _(g, b_=b_):
                    g = Ser(env, "g", g)
                    g.dma_start(wbig[:, :].rearrange("p (c n) -> p c n", c=8),
                                kvgeow_ap.rearrange("(c p) n -> p c n", p=128)
                                ).then_inc(S["gw"], 16)
                @blk.tensor
                def _(t, b_=b_):
                    t.wait_ge(S["gw"], B["gw"] + 128 * b_ + 112)
                    for i in range(16):
                        if i >= 1:
                            t.wait_ge(S["v4c"], B["v4c"] + b_ * 32 + 2 * i)
                        for r, psq in enumerate([psA, psB, psC, psD]):
                            half = r % 2
                            isv = r // 2
                            ins = None
                            for dc in range(8):
                                ins = t.matmul(
                                    psq[:, :],
                                    xlc(dc, 1 + i * 128, 128),
                                    wbig[:, dc * 2048 + isv * 1024 + half * 512:
                                         dc * 2048 + isv * 1024 + half * 512 + 512],
                                    start=(dc == 0), stop=(dc == 7))
                            ins.then_inc(S["t4c"], 1)
                        if i >= 1:
                            w = i - 1
                            t.wait_ge(S["v4c"], B["v4c"] + b_ * 32 + 2 * w + 2)
                            ins = None
                            for dc in range(8):
                                ins = t.matmul(
                                    psEF[w % 2][:, dc * 128:(dc + 1) * 128],
                                    scrA[:, w * 1024 + dc * 128:
                                         w * 1024 + dc * 128 + 128],
                                    env.ident_bf[:, :], is_transpose=True,
                                    start=True, stop=True, skip_group_check=True)
                            ins.then_inc(S["t4x"], 1)
                    w = 15
                    t.wait_ge(S["v4c"], B["v4c"] + b_ * 32 + 2 * w + 2)
                    ins = None
                    for dc in range(8):
                        ins = t.matmul(psEF[w % 2][:, dc * 128:(dc + 1) * 128],
                                       scrA[:, w * 1024 + dc * 128:
                                            w * 1024 + dc * 128 + 128],
                                       env.ident_bf[:, :], is_transpose=True,
                                       start=True, stop=True,
                                       skip_group_check=True)
                    ins.then_inc(S["t4x"], 1)
                @blk.vector
                def _(v, b_=b_):
                    v = Ser(env, "v", v)
                    for i in range(16):
                        v.wait_ge(S["t4c"], B["t4c"] + (b_ * 16 + i + 1) * 4)
                        v.tensor_copy(kgeo_t[:, 0:512], psA[:, :])
                        v.tensor_copy(kgeo_t[:, 512:1024], psB[:, :])
                        v.tensor_copy(vgeo_t[:, 0:512], psC[:, :])
                        v.tensor_copy(vgeo_t[:, 512:1024], psD[:, :],
                                      inc=(S["v4c"], 1, B["v4c"] + b_ * 32 + 2 * i + 1))
                        for h in range(H0):
                            v.scalar_tensor_tensor(
                                scrA[:, i * 1024 + h * 64:i * 1024 + h * 64 + 64],
                                kgeo_t[:, h * 64:(h + 1) * 64],
                                gates[:, i * 16 + h:i * 16 + h + 1],
                                scrA[:, i * 1024 + h * 64:i * 1024 + h * 64 + 64],
                                op0=OP.mult, op1=OP.add)
                            v.scalar_tensor_tensor(
                                vE[:, i * 1024 + h * 64:i * 1024 + h * 64 + 64],
                                vgeo_t[:, h * 64:(h + 1) * 64],
                                gates[:, i * 16 + h:i * 16 + h + 1],
                                vE[:, i * 1024 + h * 64:i * 1024 + h * 64 + 64],
                                op0=OP.mult, op1=OP.add,
                                inc=((S["v4c"], 1, B["v4c"] + b_ * 32 + 2 * i + 2)
                                     if h == H0 - 1 else None))
                        if i >= 1:
                            w = i - 1
                            v.wait_ge(S["t4x"], B["t4x"] + b_ * 16 + w + 1)
                            full = kT[:, :].rearrange("p (c w2) -> p c w2", c=8)
                            v.tensor_copy(full[:, :, w * 128:(w + 1) * 128],
                                          psEF[w % 2][:, :].rearrange(
                                              "p (c w2) -> p c w2", c=8),
                                          inc=(S["v4x"], 1, B["v4x"] + b_ * 16 + w + 1))
                    w = 15
                    v.wait_ge(S["t4x"], B["t4x"] + b_ * 16 + w + 1)
                    full = kT[:, :].rearrange("p (c w2) -> p c w2", c=8)
                    v.tensor_copy(full[:, :, w * 128:(w + 1) * 128],
                                  psEF[w % 2][:, :].rearrange(
                                      "p (c w2) -> p c w2", c=8),
                                  inc=(S["v4x"], 1, B["v4x"] + b_ * 16 + w + 1))

            with nc.Block(f"{tag}_b{b_}_mt") as blk:
                @blk.scalar
                def _(sc, b_=b_):
                    sc = Ser(env, "s", sc)
                    for c in range(8):
                        if c >= 1:
                            sc.wait_ge(S["t4m"], B["t4m"] + b_ * 33 + 4 * c)
                        sc.activation(sqk[:, :], kT[:, c * 2048:(c + 1) * 2048],
                                      AF.Square, inc=(S["s4m"], 1,
                                                      B["s4m"] + b_ * 9 + c + 1))
                    sc.wait_ge(S["v4m"], B["v4m"] + b_ * 34 + 33)
                    sc.activation(mrow[0:1, 32:48], mrow[0:1, 32:48], AF.Ln)
                    sc.activation(mrow[0:1, 32:48], mrow[0:1, 32:48], AF.Exp,
                                  scale=0.5, inc=(S["s4m"], 1, B["s4m"] + b_ * 9 + 9))
                @blk.tensor
                def _(t, b_=b_):
                    for c in range(8):
                        t.wait_ge(S["s4m"], B["s4m"] + b_ * 9 + c + 1)
                        for st4 in range(4):
                            idx = c * 4 + st4
                            if idx >= 1:
                                t.wait_ge(S["v4m"], B["v4m"] + b_ * 34 + idx)
                            t.matmul(psC[0:1, 0:512], env.onescol_bf[0:64, 0:1],
                                     sqk[0:64, st4 * 512:(st4 + 1) * 512],
                                     start=True, stop=True, skip_group_check=True)
                            t.matmul(psD[0:1, 0:512], env.onescol_bf[64:128, 0:1],
                                     sqk[64:128, st4 * 512:(st4 + 1) * 512],
                                     tile_position=(64, 0),
                                     start=True, stop=True, skip_group_check=True
                                     ).then_inc(S["t4m"], 1)
                    t.wait_ge(S["s4m"], B["s4m"] + b_ * 9 + 9)
                    t.matmul(psC[:, 0:16], env.onesrow_f[:, :],
                             mrow[0:1, 32:48], start=True, stop=True
                             ).then_inc(S["t4m"], 1)
                @blk.vector
                def _(v, b_=b_):
                    v = Ser(env, "v", v)
                    for c in range(8):
                        for st4 in range(4):
                            idx = c * 4 + st4
                            v.wait_ge(S["t4m"], B["t4m"] + b_ * 33 + idx + 1)
                            v.reduce_max(ell[0:1, 0:1], psC[0:1, 0:512], axis=AX.X)
                            v.tensor_tensor(mrow[0:1, 16 + 2 * c:17 + 2 * c],
                                            mrow[0:1, 16 + 2 * c:17 + 2 * c],
                                            ell[0:1, 0:1], op=OP.max)
                            v.reduce_max(ell[0:1, 0:1], psD[0:1, 0:512], axis=AX.X)
                            v.tensor_tensor(mrow[0:1, 17 + 2 * c:18 + 2 * c],
                                            mrow[0:1, 17 + 2 * c:18 + 2 * c],
                                            ell[0:1, 0:1], op=OP.max,
                                            inc=(S["v4m"], 1,
                                                 B["v4m"] + b_ * 34 + idx + 1))
                    v.tensor_scalar(mrow[0:1, 0:16], mrow[0:1, 0:16], 1e-30, None,
                                    op0=OP.max)
                    v.tensor_scalar(mrow[0:1, 16:32], mrow[0:1, 16:32], 1e-30,
                                    None, op0=OP.max)
                    v.tensor_tensor(mrow[0:1, 32:48], mrow[0:1, 0:16],
                                    mrow[0:1, 16:32], op=OP.mult,
                                    inc=(S["v4m"], 1, B["v4m"] + b_ * 34 + 33))
                    v.wait_ge(S["t4m"], B["t4m"] + b_ * 33 + 33)
                    v.tensor_scalar(mrep[:, :], psC[:, 0:16], -SCALE0, None,
                                    op0=OP.mult,
                                    inc=(S["v4m"], 1, B["v4m"] + b_ * 34 + 34))
            with nc.Block(f"{tag}_b{b_}_at") as blk:
                @blk.sync
                def _(s, b_=b_):
                    d_qo.wait(s, b_ * 32 + 30)
                    d_qo.wait(s, b_ * 32 + 31)
                    for i in range(16):
                        gi = b_ * 16 + i
                        if i >= 1:
                            t5sv = B["t5s"] + b_ * 640 + 16 * CNS[i]
                            s.wait_ge(S["t5s"], t5sv)
                        d_ql.start(s, qtile[:, :]
                                   .rearrange("p (c n) -> p c n", c=8),
                                   qT_hbm[i].rearrange("c p n -> p c n"), gi)
                        if i >= 1:
                            d_st.wait(s, b_ * 16 + i - 1)
                        d_lo.start(s, xr[:, 0:1024], xtile(x_src, b_, i), gi)
                @blk.gpsimd
                def _(g, b_=b_):
                    g = Ser(env, "g", g)
                    g.dma_start(wbig[:, 0:8192].rearrange("p (c n) -> p c n", c=8),
                                outw_ap.rearrange("(c p) n -> p c n", p=128)
                                ).then_inc(S["gw"], 16)
                @blk.tensor
                def _(t, b_=b_):
                    for i in range(16):
                        gi = b_ * 16 + i
                        ns, pk = nsf(i), pkf(i)
                        tb_s = B["t5s"] + b_ * 640 + 16 * CNS[i]
                        tb_e = B["s5e"] + b_ * 640 + 16 * CNS[i]
                        tb_p = B["t5p"] + b_ * 384 + 16 * CPK[i]
                        tb_vp = B["v5p"] + b_ * 384 + 16 * CPK[i]
                        tb_a = B["t5a"] + b_ * 256 + i * 16
                        tb_sa = B["s5a"] + b_ * 256 + i * 16
                        d_ql.wait(t, gi)
                        for h in range(H0):
                            hb = 64 * (h % 2)
                            hc = h // 2
                            for js in range(ns):
                                e = h * ns + js
                                wj = 512 if js < i // 4 else (i % 4 + 1) * 128
                                if e >= 2:
                                    t.wait_ge(S["s5e"], tb_e + e - 1)
                                elif gi >= 1 and e == 0:
                                    t.wait_ge(S["v5o"], B["v5o"] + b_ * 32 + 2 * i)
                                ins = t.matmul(
                                    psAB[e % 2][:, 0:wj],
                                    qtile[hb:hb + 64,
                                          hc * 128:hc * 128 + 128],
                                    kT[hb:hb + 64, hc * 2048 + js * 512:
                                       hc * 2048 + js * 512 + wj],
                                    start=True, stop=True)
                                ins.then_inc(S["t5s"], 1)
                            for p_ in range(pk):
                                gpk = h * pk + p_
                                t.wait_ge(S["s5e"], tb_e + h * ns + ns)
                                if gpk >= 2:
                                    t.wait_ge(S["v5p"], tb_vp + gpk - 1)
                                lo = p_ * 8
                                hi = min(i + 1, lo + 8)
                                ins = None
                                for jc in range(lo, hi):
                                    ins = t.matmul(
                                        psEF[p_ % 2][:, (jc - lo) * 128:
                                                     (jc - lo) * 128 + 128],
                                        P_v[h % 2][:, jc * 128:(jc + 1) * 128],
                                        env.ident_bf[:, :], is_transpose=True,
                                        start=True, stop=True,
                                        skip_group_check=True)
                                ins.then_inc(S["t5p"], 1)
                            t.wait_ge(S["v5p"], tb_vp + (h + 1) * pk)
                            if h >= 2:
                                t.wait_ge(S["s5a"], tb_sa + h - 1)
                            ins = None
                            for jc in range(i + 1):
                                ins = t.matmul(
                                    psC[:, (h % 2) * 64:(h % 2) * 64 + 64],
                                    PT_v[h % 2][:, jc * 128:(jc + 1) * 128],
                                    vE[:, jc * 1024 + h * 64:jc * 1024 + h * 64 + 64],
                                    start=(jc == 0), stop=(jc == i))
                            ins.then_inc(S["t5a"], 1)
                        t.wait_ge(S["s5a"], tb_sa + 16)
                        if gi >= 1:
                            t.wait_ge(S["v5x"], B["v5x"] + gi)
                        ins = None
                        for dc in range(8):
                            ins = t.matmul(psG[:, dc * 128:(dc + 1) * 128],
                                           ao_v[:, dc * 128:dc * 128 + 128],
                                           env.ident_bf[:, :], is_transpose=True,
                                           start=True, stop=True,
                                           skip_group_check=True)
                        ins.then_inc(S["t5x"], 1)
                        t.wait_ge(S["v5x"], B["v5x"] + gi + 1)
                        if i == 0:
                            t.wait_ge(S["gw"], B["gw"] + 128 * b_ + 128)
                        for half in range(2):
                            for dc in range(8):
                                t.matmul(psAB[half][:, :],
                                         aoT_v[:, dc * 128:(dc + 1) * 128],
                                         wbig[:, dc * 1024 + half * 512:
                                              dc * 1024 + half * 512 + 512],
                                         start=(dc == 0), stop=False)
                            t.matmul(psAB[half][:, :], env.onesrow_bf[:, :],
                                     env.outb_row[0:1, half * 512:half * 512 + 512],
                                     start=False, stop=True).then_inc(S["t5o"], 1)
                @blk.vector
                def _(v, b_=b_):
                    v = Ser(env, "v", v)
                    for i in range(16):
                        gi = b_ * 16 + i
                        ns, pk = nsf(i), pkf(i)
                        tb_s = B["t5s"] + b_ * 640 + 16 * CNS[i]
                        tb_e = B["s5e"] + b_ * 640 + 16 * CNS[i]
                        tb_p = B["t5p"] + b_ * 384 + 16 * CPK[i]
                        tb_vp = B["v5p"] + b_ * 384 + 16 * CPK[i]
                        for h in range(H0):
                            e = h * ns + ns - 1
                            wj = (i % 4 + 1) * 128
                            v.wait_ge(S["t5s"], tb_s + e + 1)
                            v.tensor_tensor(psAB[e % 2][:, wj - 128:wj],
                                            psAB[e % 2][:, wj - 128:wj],
                                            env.trineg[:, :], op=OP.add,
                                            inc=(S["v5m"], 1,
                                                 B["v5m"] + b_ * 256 + i * 16 + h + 1))
                            for p_ in range(pk):
                                gpk = h * pk + p_
                                v.wait_ge(S["t5p"], tb_p + gpk + 1)
                                lo = p_ * 8
                                hi = min(i + 1, lo + 8)
                                v.tensor_copy(PT_v[h % 2][:, lo * 128:hi * 128],
                                              psEF[p_ % 2][:, 0:(hi - lo) * 128],
                                              inc=(S["v5p"], 1, tb_vp + gpk + 1))
                            v.wait_ge(S["s5e"], tb_e + h * ns + ns)
                            if b_ * 256 + i * 16 + h >= 1:
                                v.wait_ge(S["s5a"], B["s5a"] + b_ * 256 + i * 16 + h)
                            if ns > 1:
                                v.tensor_tensor(ell[:, 1:2], ellp[:, 0:1],
                                                ellp[:, 1:2], op=OP.add)
                                for js2 in range(2, ns):
                                    v.tensor_tensor(ell[:, 1:2], ell[:, 1:2],
                                                    ellp[:, js2:js2 + 1], op=OP.add)
                                v.reciprocal(ell[:, 0:1], ell[:, 1:2],
                                             inc=(S["v5l"], 1,
                                                  B["v5l"] + b_ * 256 + i * 16 + h + 1))
                            else:
                                v.reciprocal(ell[:, 0:1], ellp[:, 0:1],
                                             inc=(S["v5l"], 1,
                                                  B["v5l"] + b_ * 256 + i * 16 + h + 1))
                        v.wait_ge(S["t5x"], B["t5x"] + gi + 1)
                        v.tensor_copy(aoT_v[:, :], psG[:, 0:1024],
                                      inc=(S["v5x"], 1, B["v5x"] + gi + 1))
                        for half in range(2):
                            v.wait_ge(S["t5o"], B["t5o"] + b_ * 32 + 2 * i + half + 1)
                            d_lo.wait(v, gi)
                            dst = xr[:, half * 512:half * 512 + 512]
                            v.tensor_tensor(dst, dst, psAB[half][:, :], op=OP.add,
                                            inc=(S["v5o"], 1,
                                                 B["v5o"] + b_ * 32 + 2 * i + half + 1))
                @blk.scalar
                def _(sc, b_=b_):
                    sc = Ser(env, "s", sc)
                    for i in range(16):
                        gi = b_ * 16 + i
                        ns, pk = nsf(i), pkf(i)
                        tb_s = B["t5s"] + b_ * 640 + 16 * CNS[i]
                        tb_e = B["s5e"] + b_ * 640 + 16 * CNS[i]
                        tb_p = B["t5p"] + b_ * 384 + 16 * CPK[i]
                        for h in range(H0):
                            if h >= 2:
                                sc.wait_ge(S["t5p"], tb_p + (h - 1) * pk)
                            for js in range(ns):
                                e = h * ns + js
                                wj = 512 if js < i // 4 else (i % 4 + 1) * 128
                                sc.wait_ge(S["t5s"], tb_s + e + 1)
                                if js == 0 and b_ * 256 + i * 16 + h >= 1:
                                    sc.wait_ge(S["v5l"],
                                               B["v5l"] + b_ * 256 + i * 16 + h)
                                if js == ns - 1:
                                    sc.wait_ge(S["v5m"],
                                               B["v5m"] + b_ * 256 + i * 16 + h + 1)
                                sc.activation(P_v[h % 2][:, js * 512:js * 512 + wj],
                                              psAB[e % 2][:, 0:wj], AF.Exp,
                                              scale=SCALE0, bias=mrep[:, h:h + 1],
                                              accum_out=ellp[:, js:js + 1],
                                              inc=(S["s5e"], 1, tb_e + e + 1))
                            sc.wait_ge(S["v5l"],
                                       B["v5l"] + b_ * 256 + i * 16 + h + 1)
                            sc.wait_ge(S["t5a"], B["t5a"] + b_ * 256 + i * 16 + h + 1)
                            if gi * 16 + h >= 1 and h == 0:
                                sc.wait_ge(S["t5x"], B["t5x"] + gi)
                            sc.activation(ao_v[:, h * 64:(h + 1) * 64],
                                          psC[:, (h % 2) * 64:(h % 2) * 64 + 64],
                                          AF.Identity, scale=ell[:, 0:1],
                                          inc=(S["s5a"], 1,
                                               B["s5a"] + b_ * 256 + i * 16 + h + 1))
                        sc.wait_ge(S["v5o"], B["v5o"] + b_ * 32 + 2 * (i + 1))
                        d_st.start(sc, xtile(x_dst, b_, i),
                                   xr[:, 0:1024], b_ * 16 + i)

        with nc.Block(f"{tag}_fin") as blk:
            @blk.scalar
            def _(sc):
                d_st.wait_all(sc)
            @blk.sync
            def _(s):
                d_qo.wait_all(s)

        d_lx.done()
        d_lo.done()
        d_ql.done()
        d_st.done()
        d_qo.done()
        per_b = {"vln": 16, "slnsq": 16, "vpre": 16, "srstd": 16, "txpd": 16,
                 "vxpe": 16, "t2": 16, "v2": 18, "t3": 16, "v3": 17, "s3g": 3,
                 "t4a": 32, "v4a": 32, "s4a": 32, "t4n": 32, "v4n": 32,
                 "t4k": 64, "v4k": 16, "t4c": 64, "v4c": 32, "t4x": 16,
                 "v4x": 16, "t4m": 33, "v4m": 34, "s4m": 9,
                 "t5s": 640, "v5m": 256, "s5e": 640, "v5l": 256, "t5p": 384,
                 "v5p": 384, "t5a": 256, "s5a": 256, "t5x": 16, "v5x": 16,
                 "t5o": 32, "v5o": 32}
        for k, cnt in per_b.items():
            env.base["oa_" + k] += cnt * nb
        env.base["oa_ldx"] = env.base.get("oa_ldx", 0)
        env.base["oa_gw"] += 128 * nb
        env.base["oa_v3g"] += 0


def build_l0attn_program(nb):
    nc = bass.Bass("TRN2", target_bir_lowering=False)
    x_in = nc.dram_tensor("x_in", [nb * T, D], F32, kind="ExternalInput").ap()
    qkvw = nc.dram_tensor("L0_attn_qkv_w", [D, 3 * D], F32, kind="ExternalInput").ap()
    qkvb = nc.dram_tensor("L0_attn_qkv_b", [3 * D], F32, kind="ExternalInput").ap()
    w1w = nc.dram_tensor("L0_attn_w1_write", [D, 64], F32, kind="ExternalInput").ap()
    w2w = nc.dram_tensor("L0_attn_w2_write", [D, 64], F32, kind="ExternalInput").ap()
    r1 = nc.dram_tensor("L0_attn_w1_read", [D, 64], F32, kind="ExternalInput").ap()
    r2 = nc.dram_tensor("L0_attn_w2_read", [D, 64], F32, kind="ExternalInput").ap()
    kvgeo = nc.dram_tensor("L0_attn_kv_geo_w", [D, 2 * D], F32,
                           kind="ExternalInput").ap()
    outw = nc.dram_tensor("L0_attn_out_w", [D, D], F32, kind="ExternalInput").ap()
    outb = nc.dram_tensor("L0_attn_out_b", [D], F32, kind="ExternalInput").ap()
    kvg = nc.dram_tensor("L0_attn_kv_gate", [1], F32, kind="ExternalInput").ap()
    ms = nc.dram_tensor("L0_attn_mem_scale", [H0], F32, kind="ExternalInput").ap()
    dl = nc.dram_tensor("L0_attn_decay_logits", [H0], F32,
                        kind="ExternalInput").ap()
    x_out = nc.dram_tensor("x_out", [nb * T, D], F32, kind="ExternalOutput").ap()
    with ExitStack() as stack:
        env = Env(nc, stack)
        emit_consts(env)
        emit_l0attn_prep(env, dl, ms, kvg, qkvb, outb)
        emit_l0attn(env, x_in, x_out, nb, "at0", qkvw, kvgeo, outw,
                    w1w, w2w, r1, r2)
    return nc






_PROG_CACHE = {}


def _get_prog(name, builder):
    if name not in _PROG_CACHE:
        _PROG_CACHE[name] = builder()
    return _PROG_CACHE[name]


def kernel(**inputs):
    TOK = BPC * T
    x = np.ascontiguousarray(np.asarray(inputs["L0_attn_x"], dtype=np.float32)
                             .reshape(B * T, D))
    shards = [x[c * TOK:(c + 1) * TOK] for c in range(NCORES)]
    cores = list(range(NCORES))

    w_l0ffn = {k: np.ascontiguousarray(np.asarray(inputs[k], np.float32))
               for k in ["L0_ffn_W_fc", "L0_ffn_ln_w", "L0_ffn_ln_b",
                         "L0_ffn_W_proj"]}
    w_l0att = {k: np.ascontiguousarray(np.asarray(inputs[k], np.float32))
               for k in ["L0_attn_qkv_w", "L0_attn_qkv_b", "L0_attn_w1_write",
                         "L0_attn_w2_write", "L0_attn_w1_read",
                         "L0_attn_w2_read", "L0_attn_kv_geo_w", "L0_attn_out_w",
                         "L0_attn_out_b", "L0_attn_mem_scale",
                         "L0_attn_decay_logits"]}
    w_l0att["L0_attn_kv_gate"] = np.asarray(
        inputs["L0_attn_kv_gate"], np.float32).reshape(1)
    w_l1ffn = {k: np.ascontiguousarray(np.asarray(inputs[k], np.float32))
               for k in ["L1_ffn_ws1", "L1_ffn_bs1", "L1_ffn_ws2", "L1_ffn_bs2",
                         "L1_ffn_We1", "L1_ffn_Be1", "L1_ffn_We2", "L1_ffn_Be2",
                         "L1_ffn_Wr", "L1_ffn_br"]}
    w_l1att = {k: np.ascontiguousarray(np.asarray(inputs[k], np.float32))
               for k in ["L1_attn_Wk", "L1_attn_Wq", "L1_attn_Wv",
                         "L1_attn_Wa", "L1_attn_Wb"]}

    def launch(prog, wts, xs_):
        in_maps = [dict(wts, x_in=xs_[c]) for c in cores]
        res = run_bass_kernel_spmd(prog, in_maps, core_ids=cores)
        return [np.asarray(r["x_out"]) for r in res.results]

    p_f0 = _get_prog("l0ffn", lambda: build_l0ffn_program(TOK))
    p_a0 = _get_prog("l0attn", lambda: build_l0attn_program(BPC))
    p_f1 = _get_prog("l1ffn", lambda: build_l1ffn_program(TOK))
    p_a1 = _get_prog("l1attn", lambda: build_l1attn_program(BPC))

    shards = launch(p_f0, w_l0ffn, shards)
    shards = launch(p_a0, w_l0att, shards)
    shards = launch(p_f0, w_l0ffn, shards)
    shards = launch(p_f1, w_l1ffn, shards)
    shards = launch(p_a1, w_l1att, shards)
    shards = launch(p_f1, w_l1ffn, shards)
    out = np.concatenate(shards, axis=0).reshape(B, T, D)
    return out



def emit_copy_out(env, src, dst, tok):
    nc = env.nc
    cs = env.sem("cp_s")
    b = env.base["cp_s"]
    n = tok // 512
    with nc.Block("cpout") as blk:
        @blk.sync
        def _(s):
            for i in range(n):
                s.dma_start(dst[i * 512:(i + 1) * 512, :],
                            src[i * 512:(i + 1) * 512, :]).then_inc(cs, 16)
        @blk.gpsimd
        def _(g):
            g.wait_ge(cs, b + 16 * n)
    env.base["cp_s"] += 16 * n


def build_fused_program(nstages=6, skip=(), nofree=(), bpc=BPC):
    TOK = bpc * T
    nc = bass.Bass("TRN2", target_bir_lowering=False)
    I = lambda n, sh: nc.dram_tensor(n, sh, F32, kind="ExternalInput").ap()
    x_in = I("x_in", [TOK, D])
    wfc = I("L0_ffn_W_fc", [H_FF, D]); lnw = I("L0_ffn_ln_w", [H_FF])
    lnb = I("L0_ffn_ln_b", [H_FF]); wproj = I("L0_ffn_W_proj", [D, H_FF])
    qkvw = I("L0_attn_qkv_w", [D, 3 * D]); qkvb = I("L0_attn_qkv_b", [3 * D])
    w1w = I("L0_attn_w1_write", [D, 64]); w2w = I("L0_attn_w2_write", [D, 64])
    r1 = I("L0_attn_w1_read", [D, 64]); r2 = I("L0_attn_w2_read", [D, 64])
    kvgeo = I("L0_attn_kv_geo_w", [D, 2 * D]); outw = I("L0_attn_out_w", [D, D])
    outb = I("L0_attn_out_b", [D]); kvg = I("L0_attn_kv_gate", [1])
    ms = I("L0_attn_mem_scale", [H0]); dl = I("L0_attn_decay_logits", [H0])
    ws1 = I("L1_ffn_ws1", [D, D]); bs1 = I("L1_ffn_bs1", [D])
    ws2 = I("L1_ffn_ws2", [D, D]); bs2 = I("L1_ffn_bs2", [D])
    we1 = I("L1_ffn_We1", [E_EXP, D, D]); be1 = I("L1_ffn_Be1", [E_EXP, D])
    we2 = I("L1_ffn_We2", [E_EXP, D, D]); be2 = I("L1_ffn_Be2", [E_EXP, D])
    wr = I("L1_ffn_Wr", [D, E_EXP]); br = I("L1_ffn_br", [E_EXP])
    wk = I("L1_attn_Wk", [D, D]); wq = I("L1_attn_Wq", [D, D])
    wv = I("L1_attn_Wv", [D, D]); wa = I("L1_attn_Wa", [D, D])
    wb = I("L1_attn_Wb", [D, D])
    x_out = nc.dram_tensor("x_out", [TOK, D], F32, kind="ExternalOutput").ap()
    xres = nc.dram_tensor("xres", [TOK, D], F32, kind="Internal").ap()

    with ExitStack() as stack:
        env = Env(nc, stack)
        emit_consts(env)
        emit_l0ffn_prep(env, wfc, lnw, lnb, wproj)
        env.free_phase(["v2c_", "tern_", "scp_"])
        emit_l1ffn_prep(env, we1, we2)
        env.free_phase(["l1p_"])
        emit_l1attn_prep(env, wk, wq, wv, wa, wb)
        env.free_phase(["wtf_", "fld_"])
        dst = lambda i: x_out if i == nstages else xres
        emit_l0ffn(env, x_in, dst(1), TOK, "f0a")
        if nstages == 1:
            return nc
        if "f1" not in nofree:
            env.free_phase(["f_"])
        if 2 not in skip:
            with ExitStack() as sub:
                saved = env.stack
                env.stack = sub
                emit_l0attn_prep(env, dl, ms, kvg, qkvb, outb)
                env.stack = saved
                emit_l0attn(env, xres, dst(2), bpc, "a0", qkvw, kvgeo, outw,
                            w1w, w2w, r1, r2)
            if "a" not in nofree:
                env.free_phase(["op_", "oa_", "v2c_"])
        if nstages == 2:
            return nc
        if 3 not in skip:
            emit_l0ffn(env, xres, dst(3.5 if 35 in skip else 3), TOK, "f0b")
            if "f2" not in nofree:
                env.free_phase(["f_"])
        if 35 in skip:
            emit_copy_out(env, xres, x_out, TOK)
            return nc
        if nstages == 3:
            return nc
        with ExitStack() as sub:
            emit_l1ffn_smalls(env, be1, be2, bs1, bs2, wr, br, sub)
            emit_l1ffn(env, xres, dst(4), TOK, "q1a", ws1, ws2)
        if nstages == 4:
            return nc
        env.free_phase(["q_", "l1p_", "v2c_"])
        emit_l1attn(env, xres, dst(5), bpc, "a1")
        if nstages == 5:
            return nc
        env.free_phase(["a_"])
        with ExitStack() as sub:
            emit_l1ffn_smalls(env, be1, be2, bs1, bs2, wr, br, sub)
            emit_l1ffn(env, xres, x_out, TOK, "q1b", ws1, ws2)
    return nc


def kernel(**inputs):
    TOK = BPC * T
    x = np.ascontiguousarray(np.asarray(inputs["L0_attn_x"], dtype=np.float32)
                             .reshape(B * T, D))
    cores = list(range(NCORES))
    wts = {}
    for k in ["L0_ffn_W_fc", "L0_ffn_ln_w", "L0_ffn_ln_b", "L0_ffn_W_proj",
              "L0_attn_qkv_w", "L0_attn_qkv_b", "L0_attn_w1_write",
              "L0_attn_w2_write", "L0_attn_w1_read", "L0_attn_w2_read",
              "L0_attn_kv_geo_w", "L0_attn_out_w", "L0_attn_out_b",
              "L0_attn_mem_scale", "L0_attn_decay_logits",
              "L1_ffn_ws1", "L1_ffn_bs1", "L1_ffn_ws2", "L1_ffn_bs2",
              "L1_ffn_We1", "L1_ffn_Be1", "L1_ffn_We2", "L1_ffn_Be2",
              "L1_ffn_Wr", "L1_ffn_br",
              "L1_attn_Wk", "L1_attn_Wq", "L1_attn_Wv", "L1_attn_Wa",
              "L1_attn_Wb"]:
        wts[k] = np.ascontiguousarray(np.asarray(inputs[k], np.float32))
    wts["L0_attn_kv_gate"] = np.asarray(
        inputs["L0_attn_kv_gate"], np.float32).reshape(1)
    prog = _get_prog("fused", build_fused_program)
    in_maps = [dict(wts, x_in=x[c * TOK:(c + 1) * TOK]) for c in cores]
    res = run_bass_kernel_spmd(prog, in_maps, core_ids=cores)
    out = np.concatenate([np.asarray(r["x_out"]) for r in res.results], axis=0)
    return out.reshape(B, T, D)
